# version 06
# baseline (speedup 1.0000x reference)
import math
from contextlib import ExitStack

import numpy as np
import ml_dtypes

import concourse.bass as bass
from concourse.bass import broadcast_tensor_aps
import concourse.bass_utils as _bass_utils

if not getattr(_bass_utils, "_ldw_opt_patched", False):
    _orig_run_command = _bass_utils.run_command

    def _run_command_ldw(argv, **kwargs):
        argv = ["--enable-ldw-opt=true" if a == "--enable-ldw-opt=false" else a
                for a in argv]
        return _orig_run_command(argv, **kwargs)

    _bass_utils.run_command = _run_command_ldw
    _bass_utils._ldw_opt_patched = True
import concourse.mybir as mybir
from concourse.bass_utils import run_bass_kernel_spmd

F32 = mybir.dt.float32
BF16 = mybir.dt.bfloat16
AF = mybir.ActivationFunctionType
ALU = mybir.AluOpType
AX = mybir.AxisListType.X
BF = ml_dtypes.bfloat16

NCORES = 8
B_PER_CORE = 2
SEQ = 2048
NTOK = B_PER_CORE * SEQ
D = 2048
NH = 16
HD = 128
CH = 256
NCHUNK = SEQ // CH
LDIM = 1024
E = 8
HID = [1024, 2048, 3072, 4096, 1024, 2048, 3072, 4096]
COST_LAMBDA = 1e-07
EPS = 1e-5
THALF = 2048
NTT = NTOK // 128
NEG_IOTA_BIG = 16384.0

SEMS = ("w", "g", "st", "pe", "ss", "sv", "sp")
ENGINES = ("sync", "tensor", "vector", "scalar", "gpsimd")


def _lhsT_mtiles(w):
    K, M = w.shape
    return np.ascontiguousarray(w.reshape(K, M // 128, 128).transpose(1, 0, 2))


def _rhs_ktiles(w):
    K, N = w.shape
    return np.ascontiguousarray(w.reshape(K // 128, 128, N))


def _bias_col(b):
    return np.ascontiguousarray(np.asarray(b).reshape(-1, 128).T)


def _consts():
    start = 2.0 ** (-2.0 ** (-(math.log2(NH) - 3)))
    slopes = np.array([start ** (i + 1) for i in range(NH)], dtype=np.float64)
    lam = np.exp(-slopes)
    pos = np.arange(CH, dtype=np.float64)
    dq = lam[:, None] ** (pos + 1.0)
    dk = lam[:, None] ** (CH - 1.0 - pos)
    qp = pos[None, None, :]
    kp = pos[None, :, None]
    dmaskT = np.where(qp >= kp, lam[:, None, None] ** np.maximum(qp - kp, 0.0), 0.0)
    return {
        "dmaskT": np.ascontiguousarray(
            dmaskT.reshape(NH, 2, 128, CH).astype(BF)),
        "dqb": np.ascontiguousarray(
            np.broadcast_to(dq[:, None, :], (NH, 128, CH)).astype(BF)),
        "dkcol2": np.ascontiguousarray(
            dk.reshape(NH * 2, 128).T.astype(np.float32)),
        "lam_c": (lam ** CH).astype(np.float64),
        "ident_bf": np.eye(128, dtype=BF),
        "ident_f32": np.eye(128, dtype=np.float32),
        "ones_row_bf": np.ones((1, 128), dtype=BF),
        "iota8": np.ascontiguousarray(
            np.broadcast_to(np.tile(np.arange(8, dtype=np.float32), 16),
                            (128, 128))),
        "sel8": np.ascontiguousarray(
            np.kron(np.eye(8), np.ones((1, 128))).astype(BF)),
        "negcost": (-COST_LAMBDA * np.array([2048.0 * h for h in HID], np.float32)
                    ).reshape(1, 8),
        "epscol": np.full((128, 1), EPS, np.float32),
    }


class Em:

    def __init__(self, which, eng, sems, snap):
        self.which = which
        self.eng = eng
        self.sems = sems
        self.c = dict(snap)

    def at(self, s):
        return self.c.get(s, 0)

    def _go(self, engname, fn, incs=(), waits=()):
        if self.which == engname and self.eng is not None:
            for s, v in waits:
                if v > 0:
                    self.eng.wait_ge(self.sems[s], v)
            ins = fn(self.eng)
            for s, n in incs:
                ins = ins.then_inc(self.sems[s], n)
        for s, n in incs:
            self.c[s] = self.c.get(s, 0) + n

    def dma(self, queue, out, in_, incs=(), waits=(), transpose=False):
        if transpose:
            self._go(queue, lambda e: e.dma_start(out=out, in_=in_, transpose=True),
                     incs, waits)
        else:
            self._go(queue, lambda e: e.dma_start(out=out, in_=in_), incs, waits)

    def mm(self, out, lhsT, rhs, start, stop, incs=(), waits=()):
        self._go("tensor",
                 lambda e: e.matmul(out, lhsT, rhs, start=start, stop=stop),
                 incs, waits)

    def tr(self, out, in_, ident, incs=(), waits=()):
        self._go("tensor", lambda e: e.transpose(out, in_, ident), incs, waits)

    def act(self, out, in_, func, bias=0.0, scale=1.0, accum_out=None,
            incs=(), waits=()):
        if func is AF.Copy and not isinstance(bias, float):
            func = AF.Identity
        self._go("scalar",
                 lambda e: e.activation(out, in_, func, bias=bias, scale=scale,
                                        accum_out=accum_out),
                 incs, waits)

    def tt(self, out, in0, in1, op, incs=(), waits=(), eng="vector"):
        self._go(eng, lambda e: e.tensor_tensor(out, in0, in1, op), incs, waits)

    def ts(self, out, in0, s1, op0, s2=None, op1=ALU.bypass, incs=(), waits=(),
           eng="vector"):
        self._go(eng, lambda e: e.tensor_scalar(out, in0, s1, s2, op0, op1),
                 incs, waits)

    def stt(self, out, in0, scalar, in1, op0, op1, incs=(), waits=(), eng="vector"):
        self._go(eng,
                 lambda e: e.scalar_tensor_tensor(out, in0, scalar, in1, op0, op1),
                 incs, waits)

    def cp(self, out, in_, incs=(), waits=(), eng="vector"):
        self._go(eng, lambda e: e.tensor_copy(out, in_), incs, waits)

    def red(self, out, in_, op, incs=(), waits=(), eng="vector"):
        self._go(eng, lambda e: e.tensor_reduce(out, in_, AX, op), incs, waits)

    def recip(self, out, in_, incs=(), waits=()):
        self._go("vector", lambda e: e.reciprocal(out, in_), incs, waits)

    def memset(self, ap, val, incs=(), waits=(), eng="vector"):
        self._go(eng, lambda e: e.memset(ap, val), incs, waits)

    def nop(self, eng, incs=(), waits=()):
        self._go(eng, lambda e: e.nop(), incs, waits)

    def vflush(self, p, src_slice, incs=()):
        np_, nf = src_slice.shape[0], min(int(src_slice.shape[1]), 16)
        self.cp(p.sb["vscr"][0:np_, 0:nf], src_slice[:, 0:nf], incs=incs)


class Guard:

    def __init__(self):
        self.needs = {}

    def w(self):
        return tuple(self.needs.items())

    def set(self, sem, val):
        self.needs = {sem: val}

    def add(self, sem, val):
        if self.needs.get(sem, -1) < val:
            self.needs[sem] = val

    def merge(self, other):
        for s, v in other.needs.items():
            self.add(s, v)


class P:
    def __init__(self):
        self.dram = {}
        self.sb = {}
        self.ps = {}
        self.sems = {}


def _declare_drams(nc, p, debug=False):
    d = p.dram
    dt = nc.dram_tensor
    d["x"] = dt("x", [NTOK, D], F32, kind="ExternalInput").ap()
    d["wqk4"] = dt("wqk4", [16, 128, 2 * D], F32, kind="ExternalInput").ap()
    d["wv4"] = dt("wv4", [16, 128, D], F32, kind="ExternalInput").ap()
    d["bqk_col"] = dt("bqk_col", [128, 32], F32, kind="ExternalInput").ap()
    d["bv_row"] = dt("bv_row", [1, D], F32, kind="ExternalInput").ap()
    d["woT3"] = dt("woT3", [16, 128, D], F32, kind="ExternalInput").ap()
    d["bo_row"] = dt("bo_row", [1, D], F32, kind="ExternalInput").ap()
    d["down4"] = dt("down4", [16, 128, LDIM], F32, kind="ExternalInput").ap()
    d["downb_col"] = dt("downb_col", [128, 8], F32, kind="ExternalInput").ap()
    d["router3"] = dt("router3", [16, 128, 8], F32, kind="ExternalInput").ap()
    d["rb_row"] = dt("rb_row", [1, 8], F32, kind="ExternalInput").ap()
    d["sh1_4"] = dt("sh1_4", [8, 128, LDIM], F32, kind="ExternalInput").ap()
    d["sh1b_col"] = dt("sh1b_col", [128, 8], F32, kind="ExternalInput").ap()
    d["sh2_3"] = dt("sh2_3", [8, 128, LDIM], F32, kind="ExternalInput").ap()
    d["sh2b_col"] = dt("sh2b_col", [128, 8], F32, kind="ExternalInput").ap()
    d["up3"] = dt("up3", [8, 128, D], F32, kind="ExternalInput").ap()
    d["upb_row"] = dt("upb_row", [1, D], F32, kind="ExternalInput").ap()
    d["core3"] = dt("core3", [16, 128, D], F32, kind="ExternalInput").ap()
    d["coreb_row"] = dt("coreb_row", [1, D], F32, kind="ExternalInput").ap()
    d["b2cols"] = dt("b2cols", [128, 64], F32, kind="ExternalInput").ap()
    d["b2stack"] = dt("b2stack", [8, LDIM], F32, kind="ExternalInput").ap()
    for e in range(E):
        h = HID[e]
        d[f"e{e}w1_4"] = dt(f"e{e}w1_4", [8, 128, h], F32,
                            kind="ExternalInput").ap()
        d[f"e{e}b1_col"] = dt(f"e{e}b1_col", [128, h // 128], F32,
                              kind="ExternalInput").ap()
        d[f"e{e}w2_3"] = dt(f"e{e}w2_3", [h // 128, 128, LDIM], F32,
                            kind="ExternalInput").ap()
    d["dmaskT"] = dt("dmaskT", [NH, 2, 128, CH], BF16, kind="ExternalInput").ap()
    d["dqb"] = dt("dqb", [NH, 128, CH], BF16, kind="ExternalInput").ap()
    d["dkcol2"] = dt("dkcol2", [128, NH * 2], F32, kind="ExternalInput").ap()
    d["ident_bf"] = dt("ident_bf", [128, 128], BF16, kind="ExternalInput").ap()
    d["ident_f32"] = dt("ident_f32", [128, 128], F32, kind="ExternalInput").ap()
    d["ones_row_bf"] = dt("ones_row_bf", [1, 128], BF16, kind="ExternalInput").ap()
    d["iota8"] = dt("iota8", [128, 128], F32, kind="ExternalInput").ap()
    d["sel8"] = dt("sel8", [8, 1024], BF16, kind="ExternalInput").ap()
    d["negcost"] = dt("negcost", [1, 8], F32, kind="ExternalInput").ap()
    d["epscol"] = dt("epscol", [128, 1], F32, kind="ExternalInput").ap()

    d["out"] = dt("out", [NTOK, D], F32, kind="ExternalOutput").ap()

    def scr(name, shape, dtype=BF16, dbg=False):
        kind = "ExternalOutput" if (debug and dbg) else "Internal"
        d[name] = dt(name, shape, dtype, kind=kind).ap()

    scr("xb", [NTOK, D])
    scr("down4b", [16, 128, LDIM])
    scr("sh1_4b", [8, 128, LDIM])
    scr("sh2_3b", [8, 128, LDIM])
    scr("up3b", [8, 128, D])
    scr("core3b", [16, 128, D])
    for e in range(E):
        h = HID[e]
        scr(f"e{e}w1_4b", [8, 128, h])
        scr(f"e{e}w2_3b", [h // 128, 128, LDIM])
    scr("qk_scr", [B_PER_CORE, NCHUNK, 128, 32 * CH], dbg=True)
    scr("v_scr", [NTOK, D], dbg=True)
    scr("y1_scr", [NTOK, D], F32, dbg=True)
    scr("y1b_scr", [NTOK, D], dbg=True)
    if debug:
        d["dbg_gh"] = dt("dbg_gh", [2, 128, 8 * THALF], BF16,
                         kind="ExternalOutput").ap()
        d["dbg_accsh"] = dt("dbg_accsh", [2, 128, 8 * THALF], BF16,
                            kind="ExternalOutput").ap()
        d["dbg_acc"] = dt("dbg_acc", [2, 128, 8 * THALF], BF16,
                          kind="ExternalOutput").ap()
        d["dbg_w"] = dt("dbg_w", [2, 128, 128], F32, kind="ExternalOutput").ap()
        d["dbg_wp"] = dt("dbg_wp", [2, 128, 128], F32, kind="ExternalOutput").ap()
        d["dbg_wT"] = dt("dbg_wT", [2, 8, 2048], BF16, kind="ExternalOutput").ap()
        d["dbg_cmat"] = dt("dbg_cmat", [2, 8, 1024], BF16,
                           kind="ExternalOutput").ap()
        d["dbg_v5"] = dt("dbg_v5", [32, 128, D], F32, kind="ExternalOutput").ap()
        d["dbg_s5"] = dt("dbg_s5", [32, 128, 32], F32, kind="ExternalOutput").ap()


def conv_phase(em, p, early):
    d = p.dram
    if early:
        pairs = [("x", "xb")]
    else:
        pairs = [("down4", "down4b"), ("sh1_4", "sh1_4b"), ("sh2_3", "sh2_3b"),
                 ("up3", "up3b"), ("core3", "core3b")]
        for e in range(E):
            pairs += [(f"e{e}w1_4", f"e{e}w1_4b"), (f"e{e}w2_3", f"e{e}w2_3b")]
    pieces = []
    for src, dst in pairs:
        s, t = d[src], d[dst]
        n0 = s.shape[0]
        bytes_per0 = int(np.prod(s.shape[1:])) * 4
        step = max(1, (4 << 20) // bytes_per0)
        for i0 in range(0, n0, step):
            j0 = min(n0, i0 + step)
            pieces.append((t[i0:j0], s[i0:j0]))
    if em is None:
        return pieces
    for t_, s_ in pieces:
        em.dma("gpsimd", t_, s_, incs=(("g", 16),))
    return None


def p1_load_phase(em, p, g_early):
    d, sb = p.dram, p.sb
    em.nop("sync", waits=(("g", g_early),))
    em.dma("sync", sb["bqk"][:], d["bqk_col"], incs=(("w", 16),))
    em.dma("gpsimd", sb["p1_brow"][:], d["bv_row"], incs=(("g", 16),))
    em.dma("sync", sb["ones_bf"][:], d["ones_row_bf"], incs=(("w", 16),))
    xF = sb["xF"]
    for kt in range(16):
        em.dma("sync", xF[:, kt * NTOK:(kt + 1) * NTOK],
               d["xb"][:, kt * 128:(kt + 1) * 128],
               incs=(("w", 16),), transpose=True)


def _xf(p, kt, n0, n1):
    xF = p.sb["xF"]
    return xF[:, kt * NTOK + n0: kt * NTOK + n1]


def p1_v_phase(em, p):
    d, sb, ps = p.dram, p.sb, p.ps
    wv = sb["p1_wv"]
    ebuf = sb["p1_ebuf"]
    ones_bf = sb["ones_bf"]
    brow = sb["p1_brow"]
    for kt in range(16):
        em.dma("gpsimd", wv[:, kt * D:(kt + 1) * D], d["wv4"][kt],
               incs=(("g", 16),))
    loads_done = em.at("w")
    g_done = em.at("g")
    bank_g = [Guard() for _ in range(4)]
    ebuf_g = [Guard() for _ in range(2)]
    pidx = 0
    for tt in range(NTT):
        for pi in range(2):
            bp = (pidx % 2) * 2
            pidx += 1
            banks = [ps[f"pb32_{bp}"][:, 0:512], ps[f"pb32_{bp + 1}"][:, 0:512]]
            ns = (pi * 2, pi * 2 + 1)
            for kt in range(16):
                for j, n in enumerate(ns):
                    w0 = ((("w", loads_done), ("g", g_done))
                          if (tt == 0 and kt == 0) else ())
                    em.mm(banks[j], _xf(p, kt, tt * 128, (tt + 1) * 128),
                          wv[:, kt * D + n * 512: kt * D + (n + 1) * 512],
                          start=(kt == 0), stop=False,
                          waits=(bank_g[bp + j].w() + w0) if kt == 0 else ())
            for j, n in enumerate(ns):
                em.mm(banks[j], ones_bf[:], brow[:, n * 512:(n + 1) * 512],
                      start=False, stop=True, incs=(("pe", 1),))
            pe_v = em.at("pe")
            for j, n in enumerate(ns):
                slot = n % 2
                eb = ebuf[:, slot * 512:(slot + 1) * 512]
                em.act(eb, banks[j], AF.Copy, incs=(("ss", 1),),
                       waits=(("pe", pe_v),) + ebuf_g[slot].w())
                bank_g[bp + j].set("ss", em.at("ss"))
                em.dma("sync",
                       d["v_scr"][tt * 128:(tt + 1) * 128, n * 512:(n + 1) * 512],
                       eb, incs=(("st", 16),), waits=(("ss", em.at("ss")),))
                ebuf_g[slot].set("st", em.at("st"))


def p1_qk_phase(em, p):
    d, sb, ps = p.dram, p.sb, p.ps
    wqk = sb["p1_wqk"]
    ebuf = sb["p1_ebuf"]
    bqk = sb["bqk"]
    bank_g = [Guard() for _ in range(4)]
    ebuf_g = [Guard() for _ in range(2)]
    wbuf_g = [Guard() for _ in range(2)]
    qk_pidx = [0]
    em.nop("scalar", waits=(("w", em.at("w")), ("g", em.at("g"))))
    for mg in range(8):
        buf = mg % 2
        wsl = wqk[:, buf * 16 * 512:(buf + 1) * 16 * 512]
        for kt in range(16):
            em.dma("gpsimd", wsl[:, kt * 512:(kt + 1) * 512],
                   d["wqk4"][kt, :, mg * 512:(mg + 1) * 512],
                   incs=(("g", 16),),
                   waits=wbuf_g[buf].w() if kt == 0 else ())
        w_done = ("g", em.at("g"))
        for t_, s_ in p.conv_pieces[mg * 3:(mg + 1) * 3]:
            em.dma("gpsimd", t_, s_, incs=(("g", 16),))
        for mi in range(4):
            m = mg * 4 + mi
            for pi in range(4):
                bp = (qk_pidx[0] % 2) * 2
                qk_pidx[0] += 1
                banks = [ps[f"pb32_{bp}"][:, 0:512],
                         ps[f"pb32_{bp + 1}"][:, 0:512]]
                ns = (pi * 2, pi * 2 + 1)
                for kt in range(16):
                    for j, n in enumerate(ns):
                        em.mm(banks[j],
                              wsl[:, kt * 512 + mi * 128: kt * 512 + (mi + 1) * 128],
                              _xf(p, kt, n * 512, (n + 1) * 512),
                              start=(kt == 0), stop=(kt == 15),
                              incs=((("pe", 1),) if kt == 15 else ()),
                              waits=(bank_g[bp + j].w() + (w_done,))
                              if kt == 0 else ())
                pe_qk = em.at("pe")
                for j, n in enumerate(ns):
                    slot = n % 2
                    eb = ebuf[:, slot * 512:(slot + 1) * 512]
                    em.act(eb, banks[j], AF.Copy, bias=bqk[:, m:m + 1],
                           incs=(("ss", 1),),
                           waits=(("pe", pe_qk),) + ebuf_g[slot].w())
                    bank_g[bp + j].set("ss", em.at("ss"))
                    tok0 = n * 512
                    b = tok0 // SEQ
                    for half in range(2):
                        c = (tok0 + half * 256 - b * SEQ) // CH
                        em.dma("sync",
                               d["qk_scr"][b, c, :, m * CH:(m + 1) * CH],
                               eb[:, half * 256:(half + 1) * 256],
                               incs=(("st", 16),), waits=(("ss", em.at("ss")),))
                    ebuf_g[slot].set("st", em.at("st"))
        wbuf_g[buf].set("pe", em.at("pe"))


LAM_C = _consts()["lam_c"]


def p2_load_phase(em, p):
    d, sb = p.dram, p.sb
    woT = sb["p2_woT"]
    for kt in range(16):
        em.dma("gpsimd", woT[:, kt * D:(kt + 1) * D], d["woT3"][kt],
               incs=(("g", 16),))
    dm = sb["p2_dmask"]
    for h in range(NH):
        for t in range(2):
            em.dma("sync", dm[:, (h * 2 + t) * CH:(h * 2 + t + 1) * CH],
                   d["dmaskT"][h, t], incs=(("w", 16),))
    dq = sb["p2_dqb"]
    for h in range(NH):
        em.dma("sync", dq[:, h * CH:(h + 1) * CH], d["dqb"][h],
               incs=(("w", 16),))
    em.dma("sync", sb["p2_dk"][:], d["dkcol2"], incs=(("w", 16),))
    em.dma("sync", sb["ident_sb"][:], d["ident_bf"], incs=(("w", 16),))
    em.dma("gpsimd", sb["p2_brow"][:], d["bo_row"], incs=(("g", 16),))
    em.dma("sync", sb["eps_sb"][:], d["epscol"], incs=(("w", 16),))


def p2_phase(em, p):
    d, sb, ps = p.dram, p.sb, p.ps
    woT, dm, dq, dk = sb["p2_woT"], sb["p2_dmask"], sb["p2_dqb"], sb["p2_dk"]
    ident = sb["ident_sb"]
    ones_bf = sb["ones_bf"]
    brow_o = sb["p2_brow"]
    qkc, vc, xbc = sb["p2_qkc"], sb["p2_vc"], sb["p2_xbc"]
    S = sb["p2_S"]
    qd, kdT, scT = sb["p2_qd"], sb["p2_kdT"], sb["p2_scT"]
    attnF = sb["p2_attnF"]
    vln, y1b = sb["p2_vln"], sb["p2_y1b"]
    stats, junk = sb["p2_stats"], sb["p2_junk"]

    loads_done = em.at("w")
    g_done = em.at("g")
    em.nop("tensor", waits=(("w", loads_done), ("g", g_done)))
    em.nop("vector", waits=(("w", loads_done), ("g", g_done)))
    em.nop("scalar", waits=(("w", loads_done), ("g", g_done)))

    bank_g = {k: Guard() for k in
              ("sc00", "sc01", "sc10", "sc11", "oT0", "oT1", "k0", "k1",
               "op0", "op1", "op2", "op3")}
    qkc_g = [Guard(), Guard()]
    vc_g = [Guard(), Guard()]
    xbc_g = [Guard(), Guard()]
    vln_g = [Guard(), Guard()]
    y1b_g = [Guard(), Guard()]
    junk_g = Guard()
    sbuf_g = {k: Guard() for k in ("qd", "kdT", "scT")}
    attnF_g = Guard()
    stats_g = [Guard(), Guard()]

    scb = [[ps["pb32_0"], ps["pb32_1"]], [ps["pb32_4"], ps["pb32_5"]]]
    oTb = [ps["pb32_2"], ps["pb32_3"]]
    pk = [ps["pb16_0"][:, 0:128], ps["pb16_1"][:, 0:128]]
    pop = [ps[f"pb32_{i}"][:, 0:512] for i in range(4)]

    QW = 32 * CH
    lds = {}

    def chunk_load(em, ci):
        b, c = divmod(ci, NCHUNK)
        sl = ci % 2
        em.dma("sync", qkc[:, sl * QW:(sl + 1) * QW], d["qk_scr"][b, c],
               incs=(("w", 16),), waits=qkc_g[sl].w())
        tok0 = b * SEQ + c * CH
        for t in range(2):
            em.dma("sync", vc[:, sl * 2 * D + t * D: sl * 2 * D + (t + 1) * D],
                   d["v_scr"][tok0 + t * 128: tok0 + (t + 1) * 128],
                   incs=(("w", 16),), waits=vc_g[sl].w() if t == 0 else ())
            em.dma("sync", xbc[:, sl * 2 * D + t * D: sl * 2 * D + (t + 1) * D],
                   d["xb"][tok0 + t * 128: tok0 + (t + 1) * 128],
                   incs=(("w", 16),), waits=xbc_g[sl].w() if t == 0 else ())
        lds[ci] = em.at("w")

    chunk_load(em, 0)
    for b in range(B_PER_CORE):
        em.memset(S[:], 0.0, incs=(("sv", 1),),
                  waits=(("pe", em.at("pe")),))
        for c in range(NCHUNK):
            ci = b * NCHUNK + c
            sl = ci % 2
            if ci + 1 < 16:
                chunk_load(em, ci + 1)
            for t_, s_ in p.conv_pieces[24 + ci * 3: 24 + (ci + 1) * 3]:
                em.dma("gpsimd", t_, s_, incs=(("g", 16),))
            qof = sl * QW
            vof = sl * 2 * D
            tok0 = b * SEQ + c * CH
            ld = lds[ci]
            sv_S = em.at("sv")
            for bi, key in ((0, "sc00"), (1, "sc01"), (2, "oT0"), (3, "oT1")):
                bank_g[key].merge(bank_g[f"op{bi}"])

            for h in range(NH):
                par = h % 2
                psc = [scb[par][0][:, 0:256], scb[par][1][:, 0:256]]
                pS = scb[par][1][:, 256:384]
                poT = oTb[par][:, 0:256]
                kf = lambda t: qkc[:, qof + (16 + h) * CH + t * 128:
                                   qof + (16 + h) * CH + (t + 1) * 128]
                qf = qkc[:, qof + h * CH: qof + (h + 1) * CH]
                vt = lambda t: vc[:, vof + t * D + h * 128:
                                  vof + t * D + (h + 1) * 128]
                Sh = S[:, h * 128:(h + 1) * 128]

                for t in range(2):
                    em.tr(pk[t], kf(t), ident[:], incs=(("pe", 1),),
                          waits=(bank_g[f"k{t}"].w() + ((("w", ld),)
                                 if h == 0 else ())))
                pe_k = em.at("pe")
                for t in range(2):
                    em.act(kdT[:, t * 128:(t + 1) * 128], pk[t], AF.Copy,
                           scale=dk[:, (h * 2 + t):(h * 2 + t + 1)],
                           incs=(("ss", 1),),
                           waits=(("pe", pe_k),) + sbuf_g["kdT"].w())
                    bank_g[f"k{t}"].set("ss", em.at("ss"))
                em.tt(qd[:], qf, dq[:, h * CH:(h + 1) * CH], ALU.mult,
                      incs=(("sv", 1),),
                      waits=sbuf_g["qd"].w()
                      + ((("w", ld),) if h == 0 else ()))
                sv_kq = em.at("sv")

                for t in range(2):
                    em.mm(psc[t], kf(t), qf, start=True, stop=True,
                          incs=(("pe", 1),), waits=bank_g[f"sc{par}{t}"].w())
                pe_sc = em.at("pe")
                for t in range(2):
                    em.tt(scT[:, t * CH:(t + 1) * CH], psc[t],
                          dm[:, (h * 2 + t) * CH:(h * 2 + t + 1) * CH],
                          ALU.mult, incs=(("sv", 1),),
                          waits=(("pe", pe_sc),) + sbuf_g["scT"].w())
                    bank_g[f"sc{par}{t}"].set("sv", em.at("sv"))
                sv_sct = em.at("sv")

                first = True
                if c > 0:
                    em.mm(poT, Sh, qd[:], start=True, stop=False,
                          incs=(("pe", 1),),
                          waits=bank_g[f"oT{par}"].w() + (("sv", sv_kq),)
                          + (("sv", sv_S),))
                    first = False
                for t in range(2):
                    em.mm(poT, vt(t), scT[:, t * CH:(t + 1) * CH],
                          start=first, stop=(t == 1),
                          incs=(("pe", 1),),
                          waits=((("sv", sv_sct),)
                                 + (bank_g[f"oT{par}"].w() if first else ())))
                    first = False
                pe_oT = em.at("pe")

                for t in range(2):
                    em.mm(pS, kdT[:, t * 128:(t + 1) * 128], vt(t),
                          start=(t == 0), stop=(t == 1), incs=(("pe", 1),),
                          waits=((("sv", em.at("sv")), ("ss", em.at("ss"))))
                          if t == 0 else ())
                pe_S = em.at("pe")
                sbuf_g["kdT"].set("pe", pe_S)

                em.stt(Sh, Sh, float(LAM_C[h]), pS, ALU.mult, ALU.add,
                       incs=(("sv", 1),),
                       waits=(("pe", pe_S),))
                bank_g[f"sc{par}1"].set("sv", em.at("sv"))

                em.act(attnF[:, h * CH:(h + 1) * CH], poT, AF.Copy,
                       incs=(("ss", 1),),
                       waits=(("pe", pe_oT),) + attnF_g.w())
                bank_g[f"oT{par}"].set("ss", em.at("ss"))
                sbuf_g["scT"].set("pe", pe_oT)
                sbuf_g["qd"].set("pe", pe_oT)
            ss_attnF = em.at("ss")
            sv_S = em.at("sv")

            for bi, key in ((0, "sc00"), (1, "sc01"), (2, "oT0"), (3, "oT1")):
                bank_g[f"op{bi}"].merge(bank_g[key])
            for tti in range(2):
                for pi in range(2):
                    ns = (pi * 2, pi * 2 + 1)
                    bks = [pop[n] for n in ns]
                    for kt in range(16):
                        for j, n in enumerate(ns):
                            em.mm(bks[j],
                                  attnF[:, kt * CH + tti * 128: kt * CH + (tti + 1) * 128],
                                  woT[:, kt * D + n * 512: kt * D + (n + 1) * 512],
                                  start=(kt == 0), stop=False,
                                  waits=(bank_g[f"op{n}"].w()
                                         + (("ss", ss_attnF), ("sv", sv_S)))
                                  if kt == 0 else ())
                    for j, n in enumerate(ns):
                        em.mm(bks[j], ident[:],
                              xbc[:, vof + tti * D + n * 512:
                                  vof + tti * D + (n + 1) * 512],
                              start=False, stop=False)
                    for j, n in enumerate(ns):
                        em.mm(bks[j], ones_bf[:], brow_o[:, n * 512:(n + 1) * 512],
                              start=False, stop=True, incs=(("pe", 1),))
                    pe_op = em.at("pe")
                    for j, n in enumerate(ns):
                        em.act(vln[:, tti * D + n * 512: tti * D + (n + 1) * 512],
                               bks[j], AF.Copy,
                               accum_out=stats[:, tti * 4 + n: tti * 4 + n + 1],
                               incs=(("ss", 1),),
                               waits=(("pe", pe_op),) + vln_g[tti].w()
                               + (stats_g[tti].w() if n == 0 else ()))
                        bank_g[f"op{n}"].set("ss", em.at("ss"))
                ss_v = em.at("ss")
                sc_ = lambda j: stats[:, j:j + 1]
                em.tt(junk[:], vln[:, tti * D:(tti + 1) * D],
                      vln[:, tti * D:(tti + 1) * D], ALU.mult,
                      incs=(("sv", 1),), waits=(("ss", ss_v),) + junk_g.w())
                em.red(sc_(18 + tti), junk[:], ALU.add, incs=(("sv", 1),))
                junk_g.set("sv", em.at("sv"))
                em.red(sc_(16 + tti), stats[:, tti * 4: tti * 4 + 4], ALU.add,
                       incs=(("sv", 1),))
                sv_red = em.at("sv")
                stats_g[tti].set("sv", sv_red)
                em.act(sc_(20 + tti), sc_(16 + tti), AF.Copy,
                       scale=-1.0 / D, incs=(("ss", 1),),
                       waits=(("sv", sv_red),))
                em.act(sc_(22 + tti), sc_(16 + tti), AF.Copy, scale=1.0 / D,
                       incs=(("ss", 1),))
                em.act(sc_(24 + tti), sc_(18 + tti), AF.Copy, scale=1.0 / D,
                       incs=(("ss", 1),))
                ss_m = em.at("ss")
                em.stt(sc_(26 + tti), sc_(22 + tti), sc_(22 + tti),
                       sc_(24 + tti), ALU.mult, ALU.subtract,
                       incs=(("sv", 1),), waits=(("ss", ss_m),))
                em.act(sc_(28 + tti), sc_(26 + tti), AF.Sqrt, scale=-1.0,
                       bias=p.sb["eps_sb"][:, 0:1], incs=(("ss", 1),),
                       waits=(("sv", em.at("sv")),))
                em.recip(sc_(30 + tti), sc_(28 + tti), incs=(("sv", 1),),
                         waits=(("ss", em.at("ss")),))
                sv_rstd = em.at("sv")
                em.ts(vln[:, tti * D:(tti + 1) * D],
                      vln[:, tti * D:(tti + 1) * D],
                      sc_(20 + tti), ALU.add, s2=sc_(30 + tti), op1=ALU.mult,
                      incs=(("sv", 1),), waits=(("sv", sv_rstd),))
                em.cp(y1b[:, tti * D:(tti + 1) * D],
                      vln[:, tti * D:(tti + 1) * D], incs=(("sv", 1),),
                      waits=y1b_g[tti].w())
                sv_y1 = em.at("sv")
                trow = tok0 + tti * 128
                em.dma("gpsimd", d["y1_scr"][trow:trow + 128],
                       vln[:, tti * D:(tti + 1) * D], incs=(("st", 16),),
                       waits=(("sv", sv_y1),))
                vln_g[tti].set("st", em.at("st"))
                em.dma("gpsimd", d["y1b_scr"][trow:trow + 128],
                       y1b[:, tti * D:(tti + 1) * D], incs=(("st", 16),),
                       waits=(("sv", sv_y1),))
                y1b_g[tti].set("st", em.at("st"))
            pe_end = em.at("pe")
            qkc_g[sl].set("pe", pe_end)
            vc_g[sl].set("pe", pe_end)
            xbc_g[sl].set("pe", pe_end)
            attnF_g.set("pe", pe_end)


NCHUNKS_E = [h // 1024 for h in HID]


def p3a_phase(em, p, th):
    d, sb, ps = p.dram, p.sb, p.ps
    y1F, dw, gh = sb["y1F"], sb["p3_dw"], sb["gh"]
    rrw, rt = sb["p3_rrw"], sb["p3_rt"]
    ident32 = sb["p3_ident32"]
    iota = sb["p3_iota"]
    wmat, wpmat = sb["p3_wmat"], sb["p3_wpmat"]
    wTb, wpTb = sb["wTb"], sb["wpTb"]
    ones_bf = sb["ones_bf"]
    t0 = th * THALF

    for kt in range(16):
        em.dma("sync", y1F[:, kt * THALF:(kt + 1) * THALF],
               d["y1b_scr"][t0:t0 + THALF, kt * 128:(kt + 1) * 128],
               incs=(("w", 16),), transpose=True)
    for kt in range(16):
        em.dma("sync", dw[:, kt * LDIM:(kt + 1) * LDIM], d["down4b"][kt],
               incs=(("w", 16),))
    for kt in range(16):
        em.dma("gpsimd", rrw[:, kt * 8:(kt + 1) * 8], d["router3"][kt],
               incs=(("g", 16),))
    em.dma("sync", sb["p3_downb"][:], d["downb_col"], incs=(("w", 16),))
    em.dma("sync", ident32[:], d["ident_f32"], incs=(("w", 16),))
    em.dma("sync", iota[:], d["iota8"], incs=(("w", 16),))
    em.dma("sync", sb["p3_rb"][:], d["rb_row"], incs=(("w", 16),))
    em.dma("sync", sb["p3_nc"][:], d["negcost"], incs=(("w", 16),))
    ld = em.at("w")
    gld = em.at("g")
    em.nop("scalar", waits=(("w", ld), ("g", gld)))
    em.tt(sb["p3_rbf"][:], sb["p3_rb"][:], sb["p3_nc"][:], ALU.add,
          incs=(("sv", 1),), waits=(("w", ld),))
    em.cp(sb["p3_rbb"][:], sb["p3_rbf"][:], incs=(("sv", 1),),
          waits=(("sv", em.at("sv")),))
    sv_rb = em.at("sv")

    bank_g = [Guard() for _ in range(4)]
    for m in range(8):
        for pi in range(2):
            ns = (pi * 2, pi * 2 + 1)
            bks = [ps[f"pb32_{n}"][:, 0:512] for n in ns]
            for kt in range(16):
                for j, n in enumerate(ns):
                    em.mm(bks[j],
                          dw[:, kt * LDIM + m * 128: kt * LDIM + (m + 1) * 128],
                          y1F[:, kt * THALF + n * 512: kt * THALF + (n + 1) * 512],
                          start=(kt == 0), stop=(kt == 15),
                          incs=((("pe", 1),) if kt == 15 else ()),
                          waits=(bank_g[n].w() + (("w", ld),)) if kt == 0 else ())
            pe_d = em.at("pe")
            for j, n in enumerate(ns):
                em.act(gh[:, m * THALF + n * 512: m * THALF + (n + 1) * 512],
                       bks[j], AF.Gelu, bias=sb["p3_downb"][:, m:m + 1],
                       incs=(("ss", 1),), waits=(("pe", pe_d),))
                bank_g[n].set("ss", em.at("ss"))

    pr = ps["pb32_4"][:, 0:128]
    em.nop("tensor", waits=(("g", gld), ("sv", sv_rb)))
    for ttl in range(16):
        for kt in range(16):
            em.mm(pr[:, ttl * 8:(ttl + 1) * 8],
                  y1F[:, kt * THALF + ttl * 128: kt * THALF + (ttl + 1) * 128],
                  rrw[:, kt * 8:(kt + 1) * 8],
                  start=(kt == 0), stop=False)
        em.mm(pr[:, ttl * 8:(ttl + 1) * 8], ones_bf[:], sb["p3_rbb"][:],
              start=False, stop=True,
              incs=((("pe", 1),) if ttl == 15 else ()))
    pe_r = em.at("pe")

    def g16(i):
        return rt[:, 512 + i * 16: 512 + (i + 1) * 16]
    def v3(ap, w=8):
        return ap.rearrange("p (t e) -> p t e", e=w)
    def ttb(out, in0, small, op, rev=False, **kw):
        o3, a3 = v3(out), v3(in0)
        b3 = small.rearrange("p (t e) -> p t e", e=1)
        a3b, b3b = broadcast_tensor_aps(a3, b3)
        if rev:
            em.tt(o3, b3b, a3b, op, **kw)
        else:
            em.tt(o3, a3b, b3b, op, **kw)
    ex = rt[:, 0:128]
    pp = rt[:, 128:256]
    tcol = rt[:, 256:384]
    p2 = rt[:, 384:512]
    em.red(g16(0), v3(pr), ALU.max, incs=(("sv", 1),),
           waits=(("pe", pe_r), ("w", ld)))
    ttb(ex, pr, g16(0), ALU.subtract,
        incs=(("sv", 1),), waits=(("sv", em.at("sv")),))
    em.act(ex, ex, AF.Exp, incs=(("ss", 1),), waits=(("sv", em.at("sv")),))
    em.red(g16(1), v3(ex), ALU.add, incs=(("sv", 1),),
           waits=(("ss", em.at("ss")),))
    em.recip(g16(2), g16(1), incs=(("sv", 1),), waits=(("sv", em.at("sv")),))
    ttb(pp, ex, g16(2), ALU.mult,
        incs=(("sv", 1),), waits=(("sv", em.at("sv")),))
    em.red(g16(3), v3(pp), ALU.max, incs=(("sv", 1),),
           waits=(("sv", em.at("sv")),))
    ttb(tcol, pp, g16(3), ALU.is_equal,
        incs=(("sv", 1),), waits=(("sv", em.at("sv")),))
    em.stt(tcol, tcol, -NEG_IOTA_BIG, iota[:], ALU.mult, ALU.add,
           incs=(("sv", 1),), waits=(("sv", em.at("sv")),))
    em.red(g16(4), v3(tcol), ALU.min, incs=(("sv", 1),),
           waits=(("sv", em.at("sv")),))
    ttb(tcol, tcol, g16(4), ALU.is_equal,
        incs=(("sv", 1),), waits=(("sv", em.at("sv")),))
    em.stt(p2, tcol, -2.0, pp, ALU.mult, ALU.add, incs=(("sv", 1),),
           waits=(("sv", em.at("sv")),))
    em.red(g16(5), v3(p2), ALU.max, incs=(("sv", 1),),
           waits=(("sv", em.at("sv")),))
    ttb(p2, p2, g16(5), ALU.is_equal,
        incs=(("sv", 1),), waits=(("sv", em.at("sv")),))
    em.stt(p2, p2, -NEG_IOTA_BIG, iota[:], ALU.mult, ALU.add,
           incs=(("sv", 1),), waits=(("sv", em.at("sv")),))
    em.red(g16(6), v3(p2), ALU.min, incs=(("sv", 1),),
           waits=(("sv", em.at("sv")),))
    ttb(p2, p2, g16(6), ALU.is_equal,
        incs=(("sv", 1),), waits=(("sv", em.at("sv")),))
    em.tt(g16(7), g16(5), g16(3), ALU.subtract, incs=(("sv", 1),),
          waits=(("sv", em.at("sv")),))
    em.act(g16(8), g16(7), AF.Exp, incs=(("ss", 1),),
           waits=(("sv", em.at("sv")),))
    em.ts(g16(9), g16(8), 1.0, ALU.add, incs=(("sv", 1),),
          waits=(("ss", em.at("ss")),))
    em.recip(g16(10), g16(9), incs=(("sv", 1),), waits=(("sv", em.at("sv")),))
    em.tt(g16(11), g16(8), g16(10), ALU.mult, incs=(("sv", 1),),
          waits=(("sv", em.at("sv")),))
    em.tt(g16(12), g16(10), g16(11), ALU.add, incs=(("sv", 1),),
          waits=(("sv", em.at("sv")),))
    ttb(wmat[:], tcol, g16(10), ALU.mult,
        incs=(("sv", 1),), waits=(("sv", em.at("sv")),))
    ttb(p2, p2, g16(11), ALU.mult,
        incs=(("sv", 1),), waits=(("sv", em.at("sv")),))
    em.tt(wmat[:], wmat[:], p2, ALU.add, incs=(("sv", 1),),
          waits=(("sv", em.at("sv")),))
    ttb(wpmat[:], wmat[:], g16(12), ALU.subtract, rev=True,
        incs=(("sv", 1),), waits=(("sv", em.at("sv")),))
    em.vflush(p, wpmat[0:128, 0:8], incs=(("sv", 1),))
    em.vflush(p, wmat[0:128, 0:8], incs=(("sv", 1),))
    sv_w = em.at("sv")

    ptr32 = ps["pb32_5"]
    tr_g = Guard()
    for ttl in range(16):
        for wi, (src_m, dstb) in enumerate(((wmat, wTb), (wpmat, wpTb))):
            em.tr(ptr32[0:8, 0:128], src_m[:, ttl * 8:(ttl + 1) * 8],
                  ident32[:], incs=(("pe", 1),),
                  waits=(("sv", sv_w),) + tr_g.w())
            em.cp(dstb[0:8, ttl * 128:(ttl + 1) * 128], ptr32[0:8, 0:128],
                  incs=(("sv", 1),), waits=(("pe", em.at("pe")),))
            tr_g.set("sv", em.at("sv"))
            if ttl == 15:
                em.vflush(p, dstb[0:8, 0:8], incs=(("sv", 1),))


def p3b_phase(em, p, th):
    d, sb, ps = p.dram, p.sb, p.ps
    gh, acc, s1 = sb["gh"], sb["acc"], sb["p3_s1"]
    sh1, sh2 = sb["p3_sh1"], sb["p3_sh2"]
    for kt in range(8):
        em.dma("sync", sh1[:, kt * LDIM:(kt + 1) * LDIM], d["sh1_4b"][kt],
               incs=(("w", 16),))
        em.dma("sync", sh2[:, kt * LDIM:(kt + 1) * LDIM], d["sh2_3b"][kt],
               incs=(("w", 16),))
    em.dma("sync", sb["p3_sh1b"][:], d["sh1b_col"], incs=(("w", 16),))
    em.dma("sync", sb["p3_sh2b"][:], d["sh2b_col"], incs=(("w", 16),))
    ld = em.at("w")
    em.ts(sb["p3_sh2b01"][:], sb["p3_sh2b"][:], 0.1, ALU.mult,
          incs=(("sv", 1),), waits=(("w", ld),))
    sv_b = em.at("sv")
    em.nop("scalar", waits=(("w", ld), ("sv", sv_b)))
    bank_g = [Guard() for _ in range(4)]
    for m in range(8):
        for pi in range(2):
            ns = (pi * 2, pi * 2 + 1)
            bks = [ps[f"pb32_{n}"][:, 0:512] for n in ns]
            for kt in range(8):
                for j, n in enumerate(ns):
                    em.mm(bks[j],
                          sh1[:, kt * LDIM + m * 128: kt * LDIM + (m + 1) * 128],
                          gh[:, kt * THALF + n * 512: kt * THALF + (n + 1) * 512],
                          start=(kt == 0), stop=(kt == 7),
                          incs=((("pe", 1),) if kt == 7 else ()),
                          waits=(bank_g[n].w() + (("w", ld),)) if kt == 0 else ())
            pe_s = em.at("pe")
            for j, n in enumerate(ns):
                em.act(s1[:, m * THALF + n * 512: m * THALF + (n + 1) * 512],
                       bks[j], AF.Gelu, bias=sb["p3_sh1b"][:, m:m + 1],
                       incs=(("ss", 1),), waits=(("pe", pe_s),))
                bank_g[n].set("ss", em.at("ss"))
    ss_s1 = em.at("ss")
    for m in range(8):
        for pi in range(2):
            ns = (pi * 2, pi * 2 + 1)
            bks = [ps[f"pb32_{n}"][:, 0:512] for n in ns]
            for kt in range(8):
                for j, n in enumerate(ns):
                    em.mm(bks[j],
                          sh2[:, kt * LDIM + m * 128: kt * LDIM + (m + 1) * 128],
                          s1[:, kt * THALF + n * 512: kt * THALF + (n + 1) * 512],
                          start=(kt == 0), stop=(kt == 7),
                          incs=((("pe", 1),) if kt == 7 else ()),
                          waits=(bank_g[n].w() + (("ss", ss_s1),)
                                 + (("sv", sv_b),)) if kt == 0 else ())
            pe_s2 = em.at("pe")
            for j, n in enumerate(ns):
                em.act(acc[:, m * THALF + n * 512: m * THALF + (n + 1) * 512],
                       bks[j], AF.Copy, scale=0.1,
                       bias=sb["p3_sh2b01"][:, m:m + 1],
                       incs=(("ss", 1),), waits=(("pe", pe_s2),))
                bank_g[n].set("ss", em.at("ss"))


def p4_phase(em, p, th):
    d, sb, ps = p.dram, p.sb, p.ps
    gh, acc = sb["gh"], sb["acc"]
    z1g, fsb = sb["p4_z1g"], sb["p4_fsb"]
    w1b, w2b = sb["p4_w1b"], sb["p4_w2b"]
    wTb, wpTb = sb["wTb"], sb["wpTb"]
    wbcb = sb["p4_wbcb"]
    tmpb = sb["p4_tmp"]
    b1c, gb1 = sb["p4_b1c"], sb["p4_gb1"]
    b2c = sb["p4_b2c"]
    c_cols, c_mat = sb["p4_ccols"], sb["p4_cmat"]
    b2s = sb["p4_b2s"]
    sel8 = sb["p4_sel8"]
    ident = sb["ident_sb2"]

    em.dma("sync", b2c[:], d["b2cols"], incs=(("w", 16),))
    em.dma("gpsimd", sb["p4_b2s"][:], d["b2stack"], incs=(("g", 16),))
    em.dma("sync", sel8[:], d["sel8"], incs=(("w", 16),))
    em.dma("sync", ident[:], d["ident_bf"], incs=(("w", 16),))
    em.dma("sync", sb["p4_id32"][:], d["ident_f32"], incs=(("w", 16),))

    ld4 = em.at("w")
    em.nop("scalar", waits=(("w", ld4), ("g", em.at("g"))))
    em.nop("vector", waits=(("w", ld4), ("g", em.at("g"))))
    pz4 = [ps[f"pb32_{i}"][:, 0:512] for i in range(4)]
    pf4 = pz4
    pw = ps["pb32_4"][:, 0:512]
    pc = ps["pb32_5"][:, 0:32]
    zbank_g = [Guard() for _ in range(4)]
    fbank_g = zbank_g
    pw_g = Guard()
    pc_g = Guard()
    w1_g, w2_g, b1_g, gb1_g = Guard(), Guard(), Guard(), Guard()
    z1g_g, fsb_g = Guard(), Guard()
    tmp_g = [Guard(), Guard()]
    wbcb_g = Guard()
    ccols_g = Guard()

    for e in range(E):
        h = HID[e]
        nch = NCHUNKS_E[e]
        em.dma("sync", b1c[:, 0:h // 128], d[f"e{e}b1_col"],
               incs=(("w", 16),), waits=b1_g.w())
        em.act(gb1[:, 0:h // 128], b1c[:, 0:h // 128], AF.Gelu,
               incs=(("ss", 1),),
               waits=(("w", em.at("w")),) + gb1_g.w())
        ss_gb1 = em.at("ss")

        for n in range(4):
            em.mm(pw, sel8[:, e * 128:(e + 1) * 128],
                  wTb[:, n * 512:(n + 1) * 512], start=True, stop=True,
                  incs=(("pe", 1),), waits=pw_g.w() + (("w", em.at("w")),))
            em.act(wbcb[:, n * 512:(n + 1) * 512], pw, AF.Copy,
                   incs=(("ss", 1),),
                   waits=(("pe", em.at("pe")),) + (wbcb_g.w() if n == 0 else ()))
            pw_g.set("ss", em.at("ss"))
        ss_wbc = em.at("ss")

        for c in range(nch):
            for kt in range(8):
                em.dma("sync", w1b[:, kt * LDIM:(kt + 1) * LDIM],
                       d[f"e{e}w1_4b"][kt, :, c * LDIM:(c + 1) * LDIM],
                       incs=(("w", 16),), waits=w1_g.w() if kt == 0 else ())
            w1_ld = em.at("w")
            for m in range(8):
                for pi in range(2):
                    bp = ((m * 2 + pi) % 2) * 2
                    ns = (pi * 2, pi * 2 + 1)
                    bks = [pz4[bp], pz4[bp + 1]]
                    for kt in range(8):
                        for j, n in enumerate(ns):
                            em.mm(bks[j],
                                  w1b[:, kt * LDIM + m * 128: kt * LDIM + (m + 1) * 128],
                                  gh[:, kt * THALF + n * 512: kt * THALF + (n + 1) * 512],
                                  start=(kt == 0), stop=(kt == 7),
                                  incs=((("pe", 1),) if kt == 7 else ()),
                                  waits=(zbank_g[bp + j].w() + (("w", w1_ld),)
                                         + (z1g_g.w() if (m == 0 and pi == 0)
                                            else ())) if kt == 0 else ())
                    pe_z = em.at("pe")
                    for j, n in enumerate(ns):
                        em.act(z1g[:, m * THALF + n * 512: m * THALF + (n + 1) * 512],
                               bks[j], AF.Gelu,
                               bias=b1c[:, c * 8 + m: c * 8 + m + 1],
                               incs=(("ss", 1),), waits=(("pe", pe_z),))
                        zbank_g[bp + j].set("ss", em.at("ss"))
            w1_g.set("pe", em.at("pe"))
            ss_z1 = em.at("ss")

            for kt in range(8):
                em.dma("sync", w2b[:, kt * LDIM:(kt + 1) * LDIM],
                       d[f"e{e}w2_3b"][c * 8 + kt],
                       incs=(("w", 16),), waits=w2_g.w() if kt == 0 else ())
            w2_ld = em.at("w")
            for m2 in range(8):
                for pi in range(2):
                    fbp = ((m2 * 2 + pi) % 2) * 2
                    ns = (pi * 2, pi * 2 + 1)
                    bksf = [pf4[fbp], pf4[fbp + 1]]
                    for kt in range(8):
                        for j, n in enumerate(ns):
                            em.mm(bksf[j],
                                  w2b[:, kt * LDIM + m2 * 128: kt * LDIM + (m2 + 1) * 128],
                                  z1g[:, kt * THALF + n * 512: kt * THALF + (n + 1) * 512],
                                  start=(kt == 0), stop=(kt == 7),
                                  incs=((("pe", 1),) if kt == 7 else ()),
                                  waits=(fbank_g[fbp + j].w() + (("w", w2_ld),)
                                         + (("ss", ss_z1),)) if kt == 0 else ())
                        if pi == 0:
                            em.mm(pc[:, c * 8 + m2: c * 8 + m2 + 1],
                                  w2b[:, kt * LDIM + m2 * 128: kt * LDIM + (m2 + 1) * 128],
                                  gb1[:, c * 8 + kt: c * 8 + kt + 1],
                                  start=(kt == 0), stop=(kt == 7),
                                  incs=((("pe", 1),) if (kt == 7 and m2 == 7)
                                        else ()),
                                  waits=(("ss", ss_gb1),) + (pc_g.w()
                                         if (c == 0 and kt == 0 and m2 == 0)
                                         else ()))
                    pe_f = em.at("pe")
                    for j, n in enumerate(ns):
                        bank = bksf[j]
                        slot = (m2 * 4 + n) % 2
                        tslot = tmpb[:, slot * 512:(slot + 1) * 512]
                        dst = fsb[:, m2 * THALF + n * 512:
                                  m2 * THALF + (n + 1) * 512]
                        asl = acc[:, m2 * THALF + n * 512:
                                  m2 * THALF + (n + 1) * 512]
                        if nch == 1:
                            em.act(tslot, bank, AF.Copy,
                                   bias=b2c[:, e * 8 + m2: e * 8 + m2 + 1],
                                   incs=(("ss", 1),),
                                   waits=(("pe", pe_f),) + tmp_g[slot].w())
                            fbank_g[fbp + j].set("ss", em.at("ss"))
                            em.tt(tslot, tslot,
                                  wbcb[:, n * 512:(n + 1) * 512], ALU.mult,
                                  incs=(("sv", 1),),
                                  waits=(("ss", em.at("ss")), ("ss", ss_wbc)))
                            em.tt(asl, asl, tslot, ALU.add, incs=(("sv", 1),))
                            tmp_g[slot].set("sv", em.at("sv"))
                        elif c == 0:
                            em.act(dst, bank, AF.Copy, incs=(("ss", 1),),
                                   waits=(("pe", pe_f),)
                                   + (fsb_g.w() if (m2 == 0 and n == 0) else ()))
                            fbank_g[fbp + j].set("ss", em.at("ss"))
                        elif c < nch - 1:
                            em.act(tslot, bank, AF.Copy, incs=(("ss", 1),),
                                   waits=(("pe", pe_f),) + tmp_g[slot].w())
                            fbank_g[fbp + j].set("ss", em.at("ss"))
                            em.tt(dst, dst, tslot, ALU.add, incs=(("sp", 1),),
                                  waits=(("ss", em.at("ss")),), eng="gpsimd")
                            tmp_g[slot].set("sp", em.at("sp"))
                        else:
                            em.act(tslot, bank, AF.Copy,
                                   bias=b2c[:, e * 8 + m2: e * 8 + m2 + 1],
                                   incs=(("ss", 1),),
                                   waits=(("pe", pe_f),) + tmp_g[slot].w())
                            fbank_g[fbp + j].set("ss", em.at("ss"))
                            em.tt(tslot, tslot, dst, ALU.add, incs=(("sv", 1),),
                                  waits=(("ss", em.at("ss")), ("sp", em.at("sp"))))
                            em.tt(tslot, tslot,
                                  wbcb[:, n * 512:(n + 1) * 512], ALU.mult,
                                  incs=(("sv", 1),), waits=(("ss", ss_wbc),))
                            em.tt(asl, asl, tslot, ALU.add, incs=(("sv", 1),))
                            tmp_g[slot].set("sv", em.at("sv"))
            w2_g.set("pe", em.at("pe"))
            z1g_g.set("pe", em.at("pe"))
        fsb_g.set("sv", em.at("sv"))
        wbcb_g.set("sv", em.at("sv"))
        b1_g.set("ss", em.at("ss"))
        gb1_g.set("pe", em.at("pe"))
        em.red(c_cols[:, e * 8:(e + 1) * 8],
               pc[:, 0:nch * 8].rearrange("p (c m) -> p m c", m=8),
               ALU.add, incs=(("sv", 1),),
               waits=(("pe", em.at("pe")),))
        em.vflush(p, c_cols[:, e * 8:(e + 1) * 8], incs=(("sv", 1),))
        pc_g.set("sv", em.at("sv"))

    sv_cc = em.at("sv")
    trg = Guard()
    for m2 in range(8):
        cin = c_cols[:, :].rearrange("p (e m) -> p e m", m=8)[:, :, m2]
        em.tr(pw[0:8, 0:128], cin, sb["p4_id32"][:], incs=(("pe", 1),),
              waits=(("sv", sv_cc), ("w", em.at("w"))) + trg.w() + pw_g.w())
        em.cp(c_mat[0:8, m2 * 128:(m2 + 1) * 128], pw[0:8, 0:128],
              incs=(("sv", 1),), waits=(("pe", em.at("pe")),))
        trg.set("sv", em.at("sv"))
        pw_g.set("sv", em.at("sv"))
    em.vflush(p, c_mat[0:8, 0:8], incs=(("sv", 1),))
    sv_cm = em.at("sv")
    cb_g = [Guard(), Guard()]
    for m2 in range(8):
        for n in range(4):
            cb = (m2 * 4 + n) % 2
            bank = pz4[cb]
            em.mm(bank, c_mat[:, m2 * 128:(m2 + 1) * 128],
                  wpTb[:, n * 512:(n + 1) * 512], start=True, stop=False,
                  waits=(("sv", sv_cm), ("g", em.at("g")))
                  + cb_g[cb].w() + fbank_g[cb].w())
            em.mm(bank, b2s[:, m2 * 128:(m2 + 1) * 128],
                  wpTb[:, n * 512:(n + 1) * 512], start=False, stop=True,
                  incs=(("pe", 1),))
            em.tt(acc[:, m2 * THALF + n * 512: m2 * THALF + (n + 1) * 512],
                  acc[:, m2 * THALF + n * 512: m2 * THALF + (n + 1) * 512],
                  bank, ALU.add, incs=(("sv", 1),),
                  waits=(("pe", em.at("pe")),))
            cb_g[cb].set("sv", em.at("sv"))
            fbank_g[cb].set("sv", em.at("sv"))


def p5_phase(em, p, th, debug=False):
    d, sb, ps = p.dram, p.sb, p.ps
    acc = sb["acc"]
    upw, corew = sb["p5_upw"], sb["p5_corew"]
    gy1F = sb["p5_gy1F"]
    y1bt = sb["p5_y1bt"]
    vln, stats = sb["p5_vln"], sb["p5_stats"]
    ident = sb["ident_sb2"]
    ones_bf = sb["ones_bf"]
    ubcb = sb["ubcb"]
    t0 = th * THALF

    for kt in range(8):
        em.dma("sync", upw[:, kt * D:(kt + 1) * D], d["up3b"][kt],
               incs=(("w", 16),))
    for kt in range(16):
        em.dma("sync", corew[:, kt * D:(kt + 1) * D], d["core3b"][kt],
               incs=(("w", 16),))
    QT = 1024
    pop = [ps[f"pb32_{i}"][:, 0:512] for i in range(4)]
    bank_g = [Guard() for _ in range(4)]
    y1bt_g = [Guard(), Guard()]
    vln_g = [Guard(), Guard()]
    stats_g = [Guard(), Guard()]
    gy1_g = Guard()

    for qa in range(2):
      tq = t0 + qa * QT
      for kt in range(16):
          em.dma("sync", gy1F[:, kt * QT:(kt + 1) * QT],
                 d["y1b_scr"][tq:tq + QT, kt * 128:(kt + 1) * 128],
                 incs=(("w", 16),), transpose=True,
                 waits=gy1_g.w() if kt == 0 else ())
      ld0 = em.at("w")
      for kt in range(16):
          em.act(gy1F[:, kt * QT:(kt + 1) * QT],
                 gy1F[:, kt * QT:(kt + 1) * QT], AF.Gelu,
                 incs=(("ss", 1),), waits=((("w", ld0),) if kt == 0 else ()))
      ss_gy1 = em.at("ss")

      for ttl in range(qa * 8, qa * 8 + 8):
        tql = ttl - qa * 8
        sl = ttl % 2
        y1s = y1bt[:, sl * D:(sl + 1) * D]
        trow = t0 + ttl * 128
        em.dma("sync", y1s, d["y1b_scr"][trow:trow + 128], incs=(("w", 16),),
               waits=y1bt_g[sl].w())
        ld = em.at("w")
        for pi in range(2):
            ns = (pi * 2, pi * 2 + 1)
            bks = [pop[n] for n in ns]
            for kt in range(8):
                for j, n in enumerate(ns):
                    em.mm(bks[j],
                          acc[:, kt * THALF + ttl * 128: kt * THALF + (ttl + 1) * 128],
                          upw[:, kt * D + n * 512: kt * D + (n + 1) * 512],
                          start=(kt == 0), stop=False,
                          waits=(bank_g[n].w() + (("w", ld), ("ss", ss_gy1)))
                          if kt == 0 else ())
            for kt in range(16):
                for j, n in enumerate(ns):
                    em.mm(bks[j],
                          gy1F[:, kt * QT + tql * 128: kt * QT + (tql + 1) * 128],
                          corew[:, kt * D + n * 512: kt * D + (n + 1) * 512],
                          start=False, stop=False)
            for j, n in enumerate(ns):
                em.mm(bks[j], ident[:], y1s[:, n * 512:(n + 1) * 512],
                      start=False, stop=False)
            for j, n in enumerate(ns):
                em.mm(bks[j], ones_bf[:], ubcb[:, n * 512:(n + 1) * 512],
                      start=False, stop=True, incs=(("pe", 1),))
            pe_op = em.at("pe")
            for j, n in enumerate(ns):
                em.act(vln[:, sl * D + n * 512: sl * D + (n + 1) * 512],
                       bks[j], AF.Copy,
                       accum_out=stats[:, sl * 16 + n: sl * 16 + n + 1],
                       incs=(("ss", 1),),
                       waits=(("pe", pe_op),) + vln_g[sl].w()
                       + (stats_g[sl].w() if n == 0 else ()))
                em.act(sb["p5_junk"][:], bks[j], AF.Square,
                       accum_out=stats[:, sl * 16 + 4 + n: sl * 16 + 5 + n],
                       incs=(("ss", 1),))
                bank_g[n].set("ss", em.at("ss"))
        y1bt_g[sl].set("pe", em.at("pe"))
        ss_v = em.at("ss")
        sc_ = lambda j: stats[:, sl * 16 + j: sl * 16 + j + 1]
        em.red(sc_(8), stats[:, sl * 16: sl * 16 + 4], ALU.add,
               incs=(("sv", 1),), waits=(("ss", ss_v),))
        em.red(sc_(9), stats[:, sl * 16 + 4: sl * 16 + 8], ALU.add,
               incs=(("sv", 1),))
        sv_red = em.at("sv")
        stats_g[sl].set("sv", sv_red)
        em.act(sc_(10), sc_(8), AF.Copy, scale=-1.0 / D, incs=(("ss", 1),),
               waits=(("sv", sv_red),))
        em.act(sc_(11), sc_(8), AF.Copy, scale=1.0 / D, incs=(("ss", 1),))
        em.act(sc_(12), sc_(9), AF.Copy, scale=1.0 / D, incs=(("ss", 1),))
        ss_m = em.at("ss")
        em.stt(sc_(13), sc_(11), sc_(11), sc_(12), ALU.mult, ALU.subtract,
               incs=(("sv", 1),), waits=(("ss", ss_m),))
        em.act(sc_(14), sc_(13), AF.Sqrt, scale=-1.0,
               bias=p.sb["eps_sb"][:, 0:1], incs=(("ss", 1),),
               waits=(("sv", em.at("sv")),))
        em.recip(sc_(15), sc_(14), incs=(("sv", 1),),
                 waits=(("ss", em.at("ss")),))
        sv_rstd = em.at("sv")
        em.ts(vln[:, sl * D:(sl + 1) * D], vln[:, sl * D:(sl + 1) * D],
              sc_(10), ALU.add, s2=sc_(15), op1=ALU.mult,
              incs=(("sv", 1),), waits=(("sv", sv_rstd),))
        sv_y2 = em.at("sv")
        if debug:
            em.dma("gpsimd", d["dbg_s5"][th * 16 + ttl], stats[:],
                   incs=(("st", 16),), waits=(("sv", sv_y2),))
            stats_g[sl].set("st", em.at("st"))
        em.dma("gpsimd", d["out"][trow:trow + 128],
               vln[:, sl * D:(sl + 1) * D], incs=(("st", 16),),
               waits=(("sv", sv_y2),))
        vln_g[sl].set("st", em.at("st"))
      gy1_g.set("pe", em.at("pe"))


DEBUG = False


def _emit_block(nc, p, phase_fn, snap, engines=ENGINES):
    ends = []
    with nc.Block() as bk:
        for engname in engines:
            def body(eng, engname=engname):
                em = Em(engname, eng, p.sems, snap)
                phase_fn(em)
                ends.append(dict(em.c))
            getattr(bk, engname)(body)
    for e2 in ends[1:]:
        assert e2 == ends[0], f"sem count divergence: {ends[0]} vs {e2}"
    return ends[0]


def _barrier(nc, p, snap):
    def fn(em):
        targets = tuple((s, em.at(s)) for s in SEMS if em.at(s) > 0)
        for eng in ENGINES:
            em.nop(eng, waits=targets)
    return _emit_block(nc, p, fn, snap)


def build_nc(debug=False):
    nc = bass.Bass("TRN2", target_bir_lowering=False, debug=False)
    p = P()
    _declare_drams(nc, p, debug=debug)
    stack = ExitStack()
    with stack:
        for s in SEMS:
            p.sems[s] = stack.enter_context(nc.semaphore(s))
        for i in range(6):
            p.ps[f"pb32_{i}"] = stack.enter_context(
                nc.psum_tensor(f"pb32_{i}", [128, 512], F32))
        for i in range(2):
            p.ps[f"pb16_{i}"] = stack.enter_context(
                nc.psum_tensor(f"pb16_{i}", [128, 1024], BF16))
        p.sb["ones_bf"] = stack.enter_context(
            nc.sbuf_tensor("ones_bf", [1, 128], BF16))
        p.sb["bqk"] = stack.enter_context(nc.sbuf_tensor("bqk", [128, 32], F32))
        p.sb["eps_sb"] = stack.enter_context(
            nc.sbuf_tensor("eps_sb", [128, 1], F32))
        p.sb["ubcb"] = stack.enter_context(
            nc.sbuf_tensor("ubcb", [1, D], BF16))
        p.sb["vscr"] = stack.enter_context(
            nc.sbuf_tensor("vscr", [128, 16], F32))

        p.conv_pieces = conv_phase(None, p, False)
        assert len(p.conv_pieces) <= 24 + 48, len(p.conv_pieces)
        snap = _emit_block(nc, p, lambda em: conv_phase(em, p, True), {})
        g_early = snap["g"]

        with ExitStack() as s0:
            p.sb["i_ub"] = s0.enter_context(nc.sbuf_tensor("i_ub", [1, D], F32))
            p.sb["i_cb"] = s0.enter_context(nc.sbuf_tensor("i_cb", [1, D], F32))

            def ubcb_init(em):
                em.dma("sync", p.sb["i_ub"][:], p.dram["upb_row"],
                       incs=(("w", 16),))
                em.dma("sync", p.sb["i_cb"][:], p.dram["coreb_row"],
                       incs=(("w", 16),))
                em.tt(p.sb["i_ub"][:], p.sb["i_ub"][:], p.sb["i_cb"][:],
                      ALU.add, incs=(("sv", 1),), waits=(("w", em.at("w")),))
                em.cp(p.sb["ubcb"][:], p.sb["i_ub"][:], incs=(("sv", 1),),
                      waits=(("sv", em.at("sv")),))
            snap = _emit_block(nc, p, ubcb_init, snap)
        snap = _barrier(nc, p, snap)

        with ExitStack() as s1:
            p.sb["xF"] = s1.enter_context(
                nc.sbuf_tensor("xF", [128, 16 * NTOK], BF16))
            p.sb["p1_ebuf"] = s1.enter_context(
                nc.sbuf_tensor("p1_ebuf", [128, 2 * 512], BF16))
            p.sb["p1_brow"] = s1.enter_context(
                nc.sbuf_tensor("p1_brow", [1, D], BF16))
            snap = _emit_block(
                nc, p, lambda em: p1_load_phase(em, p, g_early), snap)
            snap = _barrier(nc, p, snap)
            with ExitStack() as s1v:
                p.sb["p1_wv"] = s1v.enter_context(
                    nc.sbuf_tensor("p1_wv", [128, 16 * D], BF16))
                snap = _emit_block(nc, p, lambda em: p1_v_phase(em, p), snap)
            snap = _barrier(nc, p, snap)
            with ExitStack() as s1q:
                p.sb["p1_wqk"] = s1q.enter_context(
                    nc.sbuf_tensor("p1_wqk", [128, 2 * 16 * 512], BF16))
                snap = _emit_block(nc, p, lambda em: p1_qk_phase(em, p), snap)

        snap = _barrier(nc, p, snap)
        with ExitStack() as s2:
            sb = p.sb
            t = nc.sbuf_tensor
            sb["p2_woT"] = s2.enter_context(t("p2_woT", [128, 16 * D], BF16))
            sb["p2_dmask"] = s2.enter_context(t("p2_dmask", [128, 32 * CH], BF16))
            sb["p2_dqb"] = s2.enter_context(t("p2_dqb", [128, 16 * CH], BF16))
            sb["p2_dk"] = s2.enter_context(t("p2_dk", [128, 32], F32))
            sb["ident_sb"] = s2.enter_context(t("ident_sb", [128, 128], BF16))
            sb["p2_brow"] = s2.enter_context(t("p2_brow", [1, D], BF16))
            sb["p2_qkc"] = s2.enter_context(t("p2_qkc", [128, 2 * 32 * CH], BF16))
            sb["p2_vc"] = s2.enter_context(t("p2_vc", [128, 4 * D], BF16))
            sb["p2_xbc"] = s2.enter_context(t("p2_xbc", [128, 4 * D], BF16))
            sb["p2_S"] = s2.enter_context(t("p2_S", [128, 16 * 128], BF16))
            sb["p2_qd"] = s2.enter_context(t("p2_qd", [128, CH], BF16))
            sb["p2_kdT"] = s2.enter_context(t("p2_kdT", [128, 2 * 128], BF16))
            sb["p2_scT"] = s2.enter_context(t("p2_scT", [128, 2 * CH], BF16))
            sb["p2_attnF"] = s2.enter_context(t("p2_attnF", [128, 16 * CH], BF16))
            sb["p2_vln"] = s2.enter_context(t("p2_vln", [128, 2 * D], F32))
            sb["p2_y1b"] = s2.enter_context(t("p2_y1b", [128, 2 * D], BF16))
            sb["p2_stats"] = s2.enter_context(t("p2_stats", [128, 32], F32))
            sb["p2_junk"] = s2.enter_context(t("p2_junk", [128, D], F32))
            snap = _emit_block(nc, p, lambda em: p2_load_phase(em, p), snap)
            snap = _emit_block(nc, p, lambda em: p2_phase(em, p), snap)

        g_late = snap["g"]

        for th in range(2):
            snap = _barrier(nc, p, snap)
            with ExitStack() as sh:
                t = nc.sbuf_tensor
                sb = p.sb
                sb["acc"] = sh.enter_context(t(f"acc{th}", [128, 8 * THALF], BF16))
                sb["ident_sb2"] = sh.enter_context(t(f"ident2_{th}", [128, 128], BF16))
                sgh = ExitStack()
                sb["gh"] = sgh.enter_context(t(f"gh{th}", [128, 8 * THALF], BF16))
                sb["wTb"] = sgh.enter_context(t(f"wTb{th}", [8, 2048], BF16))
                sb["wpTb"] = sgh.enter_context(t(f"wpTb{th}", [8, 2048], BF16))
                with ExitStack() as s3:
                    sb["y1F"] = s3.enter_context(t(f"y1F_{th}", [128, 16 * THALF], BF16))
                    sb["p3_dw"] = s3.enter_context(t(f"p3_dw_{th}", [128, 16 * LDIM], BF16))
                    sb["p3_rrw"] = s3.enter_context(t(f"p3_rrw_{th}", [128, 16 * 8], BF16))
                    sb["p3_rt"] = s3.enter_context(t(f"p3_rt_{th}", [128, 512 + 13 * 16], F32))
                    sb["p3_ident32"] = s3.enter_context(t(f"p3_ident32_{th}", [128, 128], F32))
                    sb["p3_iota"] = s3.enter_context(t(f"p3_iota_{th}", [128, 128], F32))
                    sb["p3_wmat"] = s3.enter_context(t(f"p3_wmat_{th}", [128, 128], F32))
                    sb["p3_wpmat"] = s3.enter_context(t(f"p3_wpmat_{th}", [128, 128], F32))
                    sb["p3_downb"] = s3.enter_context(t(f"p3_downb_{th}", [128, 8], F32))
                    sb["p3_rb"] = s3.enter_context(t(f"p3_rb_{th}", [1, 8], F32))
                    sb["p3_nc"] = s3.enter_context(t(f"p3_nc_{th}", [1, 8], F32))
                    sb["p3_rbf"] = s3.enter_context(t(f"p3_rbf_{th}", [1, 8], F32))
                    sb["p3_rbb"] = s3.enter_context(t(f"p3_rbb_{th}", [1, 8], BF16))
                    if th == 0:
                        def wconv(em):
                            for eng in ENGINES:
                                em.nop(eng, waits=(("g", g_late),))
                        snap = _emit_block(nc, p, wconv, snap)
                    snap = _emit_block(nc, p, lambda em: p3a_phase(em, p, th), snap)
                    if debug:
                        def dbg3a(em):
                            dd = p.dram
                            em.dma("gpsimd", dd["dbg_gh"][th], p.sb["gh"][:],
                                   incs=(("st", 16),), waits=tuple(
                                       (s, em.at(s)) for s in ("ss", "sv", "pe")))
                            em.dma("gpsimd", dd["dbg_w"][th], p.sb["p3_wmat"][:],
                                   incs=(("st", 16),))
                            em.dma("gpsimd", dd["dbg_wp"][th], p.sb["p3_wpmat"][:],
                                   incs=(("st", 16),))
                            em.dma("gpsimd", dd["dbg_wT"][th], p.sb["wTb"][:],
                                   incs=(("st", 16),))
                            em.nop("gpsimd", waits=(("st", em.at("st")),))
                        snap = _emit_block(nc, p, dbg3a, snap, engines=("gpsimd",))
                snap = _barrier(nc, p, snap)
                with ExitStack() as s3b:
                    sb["p3_s1"] = s3b.enter_context(t(f"p3_s1_{th}", [128, 8 * THALF], BF16))
                    sb["p3_sh1"] = s3b.enter_context(t(f"p3_sh1_{th}", [128, 8 * LDIM], BF16))
                    sb["p3_sh2"] = s3b.enter_context(t(f"p3_sh2_{th}", [128, 8 * LDIM], BF16))
                    sb["p3_sh1b"] = s3b.enter_context(t(f"p3_sh1b_{th}", [128, 8], F32))
                    sb["p3_sh2b"] = s3b.enter_context(t(f"p3_sh2b_{th}", [128, 8], F32))
                    sb["p3_sh2b01"] = s3b.enter_context(t(f"p3_sh2b01_{th}", [128, 8], F32))
                    snap = _emit_block(nc, p, lambda em: p3b_phase(em, p, th), snap)
                    if debug:
                        def dbg3b(em):
                            em.dma("gpsimd", p.dram["dbg_accsh"][th], p.sb["acc"][:],
                                   incs=(("st", 16),), waits=tuple(
                                       (s, em.at(s)) for s in ("ss", "sv", "pe")))
                            em.nop("gpsimd", waits=(("st", em.at("st")),))
                        snap = _emit_block(nc, p, dbg3b, snap, engines=("gpsimd",))
                snap = _barrier(nc, p, snap)
                with ExitStack() as s4:
                    sb["p4_z1g"] = s4.enter_context(t(f"p4_z1g_{th}", [128, 8 * THALF], BF16))
                    sb["p4_fsb"] = s4.enter_context(t(f"p4_fsb_{th}", [128, 8 * THALF], BF16))
                    sb["p4_w1b"] = s4.enter_context(t(f"p4_w1b_{th}", [128, 8 * LDIM], BF16))
                    sb["p4_w2b"] = s4.enter_context(t(f"p4_w2b_{th}", [128, 8 * LDIM], BF16))
                    sb["p4_wbcb"] = s4.enter_context(t(f"p4_wbcb_{th}", [128, 4 * 512], BF16))
                    sb["p4_tmp"] = s4.enter_context(t(f"p4_tmp_{th}", [128, 2 * 512], BF16))
                    sb["p4_b1c"] = s4.enter_context(t(f"p4_b1c_{th}", [128, 32], F32))
                    sb["p4_gb1"] = s4.enter_context(t(f"p4_gb1_{th}", [128, 32], BF16))
                    sb["p4_b2c"] = s4.enter_context(t(f"p4_b2c_{th}", [128, 64], F32))
                    sb["p4_ccols"] = s4.enter_context(t(f"p4_ccols_{th}", [128, 64], F32))
                    sb["p4_id32"] = s4.enter_context(t(f"p4_id32_{th}", [128, 128], F32))
                    sb["p4_b2s"] = s4.enter_context(t(f"p4_b2s_{th}", [8, LDIM], BF16))
                    sb["p4_cmat"] = s4.enter_context(t(f"p4_cmat_{th}", [8, 1024], BF16))
                    sb["p4_sel8"] = s4.enter_context(t(f"p4_sel8_{th}", [8, 1024], BF16))
                    snap = _emit_block(nc, p, lambda em: p4_phase(em, p, th), snap)
                    if debug:
                        def dbg4(em):
                            em.dma("gpsimd", p.dram["dbg_acc"][th], p.sb["acc"][:],
                                   incs=(("st", 16),), waits=tuple(
                                       (s, em.at(s)) for s in ("ss", "sv", "pe", "sp")))
                            em.dma("gpsimd", p.dram["dbg_cmat"][th], p.sb["p4_cmat"][:],
                                   incs=(("st", 16),))
                            em.nop("gpsimd", waits=(("st", em.at("st")),))
                        snap = _emit_block(nc, p, dbg4, snap, engines=("gpsimd",))
                sgh.close()
                snap = _barrier(nc, p, snap)
                if True:
                    with ExitStack() as s5:
                        sb["p5_upw"] = s5.enter_context(t(f"p5_upw_{th}", [128, 8 * D], BF16))
                        sb["p5_corew"] = s5.enter_context(t(f"p5_corew_{th}", [128, 16 * D], BF16))
                        sb["p5_y1bt"] = s5.enter_context(t(f"p5_y1bt_{th}", [128, 2 * D], BF16))
                        sb["p5_gy1F"] = s5.enter_context(t(f"p5_gy1F_{th}", [128, 16 * 1024], BF16))
                        sb["p5_vln"] = s5.enter_context(t(f"p5_vln_{th}", [128, 2 * D], F32))
                        sb["p5_junk"] = s5.enter_context(t(f"p5_junk_{th}", [128, 512], F32))
                        sb["p5_stats"] = s5.enter_context(t(f"p5_stats_{th}", [128, 32], F32))
                        snap = _emit_block(
                            nc, p, lambda em: p5_phase(em, p, th, debug=debug),
                            snap)

        def fin(em):
            em.nop("gpsimd", waits=(("st", em.at("st")),))
        snap = _emit_block(nc, p, fin, snap, engines=("gpsimd",))
    return nc


def _prep_inputs(inputs):
    c = _consts()
    g = {}
    wqkv = inputs["L0_attn_w_qkv"]
    g["wqk4"] = _rhs_ktiles(np.ascontiguousarray(wqkv[:2 * D].T))
    g["wv4"] = _rhs_ktiles(np.ascontiguousarray(wqkv[2 * D:].T))
    bqkv = inputs["L0_attn_b_qkv"]
    g["bqk_col"] = _bias_col(bqkv[:2 * D])
    g["bv_row"] = np.ascontiguousarray(bqkv[2 * D:].reshape(1, D))
    g["woT3"] = _rhs_ktiles(np.ascontiguousarray(inputs["L0_attn_w_out"].T))
    g["bo_row"] = np.ascontiguousarray(inputs["L0_attn_b_out"].reshape(1, D))
    g["down4"] = _rhs_ktiles(inputs["L0_ffn_down_w"])
    g["downb_col"] = _bias_col(inputs["L0_ffn_down_b"])
    g["router3"] = _rhs_ktiles(inputs["L0_ffn_router_w"])
    g["rb_row"] = np.ascontiguousarray(inputs["L0_ffn_router_b"].reshape(1, 8))
    g["sh1_4"] = _rhs_ktiles(inputs["L0_ffn_shared_w1"])
    g["sh1b_col"] = _bias_col(inputs["L0_ffn_shared_b1"])
    g["sh2_3"] = _rhs_ktiles(inputs["L0_ffn_shared_w2"])
    g["sh2b_col"] = _bias_col(inputs["L0_ffn_shared_b2"])
    g["up3"] = _rhs_ktiles(inputs["L0_ffn_up_w"])
    g["upb_row"] = np.ascontiguousarray(inputs["L0_ffn_up_b"].reshape(1, D))
    g["core3"] = _rhs_ktiles(inputs["L0_ffn_core_w"])
    g["coreb_row"] = np.ascontiguousarray(inputs["L0_ffn_core_b"].reshape(1, D))
    g["b2cols"] = np.concatenate(
        [_bias_col(inputs[f"L0_ffn_e{e}_b2"]) for e in range(E)], axis=1)
    g["b2stack"] = np.ascontiguousarray(
        np.stack([inputs[f"L0_ffn_e{e}_b2"] for e in range(E)]))
    for e in range(E):
        g[f"e{e}w1_4"] = _rhs_ktiles(inputs[f"L0_ffn_e{e}_w1"])
        g[f"e{e}b1_col"] = _bias_col(inputs[f"L0_ffn_e{e}_b1"])
        g[f"e{e}w2_3"] = _rhs_ktiles(inputs[f"L0_ffn_e{e}_w2"])
    for k in ("dmaskT", "dqb", "dkcol2", "ident_bf", "ident_f32", "ones_row_bf",
              "iota8", "negcost", "epscol", "sel8"):
        g[k] = c[k]

    x = inputs["L0_attn_x"]
    in_maps = []
    for ci in range(NCORES):
        m = dict(g)
        m["x"] = np.ascontiguousarray(
            x[ci * B_PER_CORE:(ci + 1) * B_PER_CORE].reshape(NTOK, D))
        in_maps.append(m)
    return in_maps


def kernel(**inputs):
    in_maps = _prep_inputs(inputs)
    nc = build_nc(debug=DEBUG)
    res = run_bass_kernel_spmd(nc, in_maps, core_ids=list(range(NCORES)))
    out = np.concatenate([r["out"] for r in res.results], axis=0)
    return out.reshape(16, SEQ, D)


# version 07
# speedup vs baseline: 1.0880x; 1.0880x over previous
import math
from contextlib import ExitStack

import numpy as np
import ml_dtypes

import concourse.bass as bass
from concourse.bass import broadcast_tensor_aps
import concourse.bass_utils as _bass_utils

if not getattr(_bass_utils, "_ldw_opt_patched", False):
    _orig_run_command = _bass_utils.run_command

    def _run_command_ldw(argv, **kwargs):
        argv = ["--enable-ldw-opt=true" if a == "--enable-ldw-opt=false" else a
                for a in argv]
        return _orig_run_command(argv, **kwargs)

    _bass_utils.run_command = _run_command_ldw
    _bass_utils._ldw_opt_patched = True
import concourse.mybir as mybir
from concourse.bass_utils import run_bass_kernel_spmd

F32 = mybir.dt.float32
BF16 = mybir.dt.bfloat16
AF = mybir.ActivationFunctionType
ALU = mybir.AluOpType
AX = mybir.AxisListType.X
BF = ml_dtypes.bfloat16

NCORES = 8
B_PER_CORE = 2
SEQ = 2048
NTOK = B_PER_CORE * SEQ
D = 2048
NH = 16
HD = 128
CH = 256
NCHUNK = SEQ // CH
LDIM = 1024
E = 8
HID = [1024, 2048, 3072, 4096, 1024, 2048, 3072, 4096]
COST_LAMBDA = 1e-07
EPS = 1e-5
THALF = 2048
NTT = NTOK // 128
NEG_IOTA_BIG = 16384.0

SEMS = ("pe", "ss", "sv", "sp",
        "cvx", "cvp", "li",
        "l1x", "l1v", "lq0", "lq1", "ze0", "ze1",
        "l2c", "lk0", "lk1", "zv0", "zv1", "zy0", "zy1",
        "l3a", "l3b",
        "l4c", "lw1", "lw2", "lb1",
        "l5", "lt0", "lt1", "zo0", "zo1", "zd")
ENGINES = ("sync", "tensor", "vector", "scalar", "gpsimd")


def _lhsT_mtiles(w):
    K, M = w.shape
    return np.ascontiguousarray(w.reshape(K, M // 128, 128).transpose(1, 0, 2))


def _rhs_ktiles(w):
    K, N = w.shape
    return np.ascontiguousarray(w.reshape(K // 128, 128, N))


def _bias_col(b):
    return np.ascontiguousarray(np.asarray(b).reshape(-1, 128).T)


def _consts():
    start = 2.0 ** (-2.0 ** (-(math.log2(NH) - 3)))
    slopes = np.array([start ** (i + 1) for i in range(NH)], dtype=np.float64)
    lam = np.exp(-slopes)
    pos = np.arange(CH, dtype=np.float64)
    dq = lam[:, None] ** (pos + 1.0)
    dk = lam[:, None] ** (CH - 1.0 - pos)
    qp = pos[None, None, :]
    kp = pos[None, :, None]
    dmaskT = np.where(qp >= kp, lam[:, None, None] ** np.maximum(qp - kp, 0.0), 0.0)
    return {
        "dmaskT": np.ascontiguousarray(
            dmaskT.reshape(NH, 2, 128, CH).astype(BF)),
        "dqb": np.ascontiguousarray(
            np.broadcast_to(dq[:, None, :], (NH, 128, CH)).astype(BF)),
        "dkcol2": np.ascontiguousarray(
            dk.reshape(NH * 2, 128).T.astype(np.float32)),
        "lam_c": (lam ** CH).astype(np.float64),
        "ident_bf": np.eye(128, dtype=BF),
        "ident_f32": np.eye(128, dtype=np.float32),
        "ones_row_bf": np.ones((1, 128), dtype=BF),
        "iota8": np.ascontiguousarray(
            np.broadcast_to(np.tile(np.arange(8, dtype=np.float32), 16),
                            (128, 128))),
        "sel8": np.ascontiguousarray(
            np.kron(np.eye(8), np.ones((1, 128))).astype(BF)),
        "negcost": (-COST_LAMBDA * np.array([2048.0 * h for h in HID], np.float32)
                    ).reshape(1, 8),
        "epscol": np.full((128, 1), EPS, np.float32),
    }


class Em:

    def __init__(self, which, eng, sems, snap):
        self.which = which
        self.eng = eng
        self.sems = sems
        self.c = dict(snap)

    def at(self, s):
        return self.c.get(s, 0)

    def _go(self, engname, fn, incs=(), waits=()):
        if self.which == engname and self.eng is not None:
            for s, v in waits:
                if v > 0:
                    self.eng.wait_ge(self.sems[s], v)
            ins = fn(self.eng)
            for s, n in incs:
                ins = ins.then_inc(self.sems[s], n)
        for s, n in incs:
            self.c[s] = self.c.get(s, 0) + n

    def dma(self, queue, out, in_, incs=(), waits=(), transpose=False):
        if transpose:
            self._go(queue, lambda e: e.dma_start(out=out, in_=in_, transpose=True),
                     incs, waits)
        else:
            self._go(queue, lambda e: e.dma_start(out=out, in_=in_), incs, waits)

    def mm(self, out, lhsT, rhs, start, stop, incs=(), waits=()):
        self._go("tensor",
                 lambda e: e.matmul(out, lhsT, rhs, start=start, stop=stop),
                 incs, waits)

    def tr(self, out, in_, ident, incs=(), waits=()):
        self._go("tensor", lambda e: e.transpose(out, in_, ident), incs, waits)

    def act(self, out, in_, func, bias=0.0, scale=1.0, accum_out=None,
            incs=(), waits=()):
        if func is AF.Copy and not isinstance(bias, float):
            func = AF.Identity
        self._go("scalar",
                 lambda e: e.activation(out, in_, func, bias=bias, scale=scale,
                                        accum_out=accum_out),
                 incs, waits)

    def tt(self, out, in0, in1, op, incs=(), waits=(), eng="vector"):
        self._go(eng, lambda e: e.tensor_tensor(out, in0, in1, op), incs, waits)

    def ts(self, out, in0, s1, op0, s2=None, op1=ALU.bypass, incs=(), waits=(),
           eng="vector"):
        self._go(eng, lambda e: e.tensor_scalar(out, in0, s1, s2, op0, op1),
                 incs, waits)

    def stt(self, out, in0, scalar, in1, op0, op1, incs=(), waits=(), eng="vector"):
        self._go(eng,
                 lambda e: e.scalar_tensor_tensor(out, in0, scalar, in1, op0, op1),
                 incs, waits)

    def cp(self, out, in_, incs=(), waits=(), eng="vector"):
        self._go(eng, lambda e: e.tensor_copy(out, in_), incs, waits)

    def red(self, out, in_, op, incs=(), waits=(), eng="vector"):
        self._go(eng, lambda e: e.tensor_reduce(out, in_, AX, op), incs, waits)

    def recip(self, out, in_, incs=(), waits=()):
        self._go("vector", lambda e: e.reciprocal(out, in_), incs, waits)

    def memset(self, ap, val, incs=(), waits=(), eng="vector"):
        self._go(eng, lambda e: e.memset(ap, val), incs, waits)

    def nop(self, eng, incs=(), waits=()):
        self._go(eng, lambda e: e.nop(), incs, waits)

    def vflush(self, p, src_slice, incs=()):
        np_, nf = src_slice.shape[0], min(int(src_slice.shape[1]), 16)
        self.cp(p.sb["vscr"][0:np_, 0:nf], src_slice[:, 0:nf], incs=incs)


class Guard:

    def __init__(self):
        self.needs = {}

    def w(self):
        return tuple(self.needs.items())

    def set(self, sem, val):
        self.needs = {sem: val}

    def add(self, sem, val):
        if self.needs.get(sem, -1) < val:
            self.needs[sem] = val

    def merge(self, other):
        for s, v in other.needs.items():
            self.add(s, v)


class P:
    def __init__(self):
        self.dram = {}
        self.sb = {}
        self.ps = {}
        self.sems = {}


def _declare_drams(nc, p, debug=False):
    d = p.dram
    dt = nc.dram_tensor
    d["x"] = dt("x", [NTOK, D], F32, kind="ExternalInput").ap()
    d["wqk4"] = dt("wqk4", [16, 128, 2 * D], F32, kind="ExternalInput").ap()
    d["wv4"] = dt("wv4", [16, 128, D], F32, kind="ExternalInput").ap()
    d["bqk_col"] = dt("bqk_col", [128, 32], F32, kind="ExternalInput").ap()
    d["bv_row"] = dt("bv_row", [1, D], F32, kind="ExternalInput").ap()
    d["woT3"] = dt("woT3", [16, 128, D], F32, kind="ExternalInput").ap()
    d["bo_row"] = dt("bo_row", [1, D], F32, kind="ExternalInput").ap()
    d["down4"] = dt("down4", [16, 128, LDIM], F32, kind="ExternalInput").ap()
    d["downb_col"] = dt("downb_col", [128, 8], F32, kind="ExternalInput").ap()
    d["router3"] = dt("router3", [16, 128, 8], F32, kind="ExternalInput").ap()
    d["rb_row"] = dt("rb_row", [1, 8], F32, kind="ExternalInput").ap()
    d["sh1_4"] = dt("sh1_4", [8, 128, LDIM], F32, kind="ExternalInput").ap()
    d["sh1b_col"] = dt("sh1b_col", [128, 8], F32, kind="ExternalInput").ap()
    d["sh2_3"] = dt("sh2_3", [8, 128, LDIM], F32, kind="ExternalInput").ap()
    d["sh2b_col"] = dt("sh2b_col", [128, 8], F32, kind="ExternalInput").ap()
    d["up3"] = dt("up3", [8, 128, D], F32, kind="ExternalInput").ap()
    d["upb_row"] = dt("upb_row", [1, D], F32, kind="ExternalInput").ap()
    d["core3"] = dt("core3", [16, 128, D], F32, kind="ExternalInput").ap()
    d["coreb_row"] = dt("coreb_row", [1, D], F32, kind="ExternalInput").ap()
    d["b2cols"] = dt("b2cols", [128, 64], F32, kind="ExternalInput").ap()
    d["b2stack"] = dt("b2stack", [8, LDIM], F32, kind="ExternalInput").ap()
    for e in range(E):
        h = HID[e]
        d[f"e{e}w1_4"] = dt(f"e{e}w1_4", [8, 128, h], F32,
                            kind="ExternalInput").ap()
        d[f"e{e}b1_col"] = dt(f"e{e}b1_col", [128, h // 128], F32,
                              kind="ExternalInput").ap()
        d[f"e{e}w2_3"] = dt(f"e{e}w2_3", [h // 128, 128, LDIM], F32,
                            kind="ExternalInput").ap()
    d["dmaskT"] = dt("dmaskT", [NH, 2, 128, CH], BF16, kind="ExternalInput").ap()
    d["dqb"] = dt("dqb", [NH, 128, CH], BF16, kind="ExternalInput").ap()
    d["dkcol2"] = dt("dkcol2", [128, NH * 2], F32, kind="ExternalInput").ap()
    d["ident_bf"] = dt("ident_bf", [128, 128], BF16, kind="ExternalInput").ap()
    d["ident_f32"] = dt("ident_f32", [128, 128], F32, kind="ExternalInput").ap()
    d["ones_row_bf"] = dt("ones_row_bf", [1, 128], BF16, kind="ExternalInput").ap()
    d["iota8"] = dt("iota8", [128, 128], F32, kind="ExternalInput").ap()
    d["sel8"] = dt("sel8", [8, 1024], BF16, kind="ExternalInput").ap()
    d["negcost"] = dt("negcost", [1, 8], F32, kind="ExternalInput").ap()
    d["epscol"] = dt("epscol", [128, 1], F32, kind="ExternalInput").ap()

    d["out"] = dt("out", [NTOK, D], F32, kind="ExternalOutput").ap()

    def scr(name, shape, dtype=BF16, dbg=False):
        kind = "ExternalOutput" if (debug and dbg) else "Internal"
        d[name] = dt(name, shape, dtype, kind=kind).ap()

    scr("xb", [NTOK, D])
    scr("down4b", [16, 128, LDIM])
    scr("sh1_4b", [8, 128, LDIM])
    scr("sh2_3b", [8, 128, LDIM])
    scr("up3b", [8, 128, D])
    scr("core3b", [16, 128, D])
    for e in range(E):
        h = HID[e]
        scr(f"e{e}w1_4b", [8, 128, h])
        scr(f"e{e}w2_3b", [h // 128, 128, LDIM])
    scr("qk_scr", [B_PER_CORE, NCHUNK, 128, 32 * CH], dbg=True)
    scr("v_scr", [NTOK, D], dbg=True)
    scr("y1_scr", [NTOK, D], F32, dbg=True)
    scr("y1b_scr", [NTOK, D], dbg=True)
    if debug:
        d["dbg_gh"] = dt("dbg_gh", [2, 128, 8 * THALF], BF16,
                         kind="ExternalOutput").ap()
        d["dbg_accsh"] = dt("dbg_accsh", [2, 128, 8 * THALF], BF16,
                            kind="ExternalOutput").ap()
        d["dbg_acc"] = dt("dbg_acc", [2, 128, 8 * THALF], BF16,
                          kind="ExternalOutput").ap()
        d["dbg_w"] = dt("dbg_w", [2, 128, 128], F32, kind="ExternalOutput").ap()
        d["dbg_wp"] = dt("dbg_wp", [2, 128, 128], F32, kind="ExternalOutput").ap()
        d["dbg_wT"] = dt("dbg_wT", [2, 8, 2048], BF16, kind="ExternalOutput").ap()
        d["dbg_cmat"] = dt("dbg_cmat", [2, 8, 1024], BF16,
                           kind="ExternalOutput").ap()
        d["dbg_v5"] = dt("dbg_v5", [32, 128, D], F32, kind="ExternalOutput").ap()
        d["dbg_s5"] = dt("dbg_s5", [32, 128, 32], F32, kind="ExternalOutput").ap()


def conv_phase(em, p, early):
    d = p.dram
    if early:
        pairs = [("x", "xb")]
    else:
        pairs = [("down4", "down4b"), ("sh1_4", "sh1_4b"), ("sh2_3", "sh2_3b"),
                 ("up3", "up3b"), ("core3", "core3b")]
        for e in range(E):
            pairs += [(f"e{e}w1_4", f"e{e}w1_4b"), (f"e{e}w2_3", f"e{e}w2_3b")]
    pieces = []
    for src, dst in pairs:
        s, t = d[src], d[dst]
        n0 = s.shape[0]
        bytes_per0 = int(np.prod(s.shape[1:])) * 4
        step = max(1, (4 << 20) // bytes_per0)
        for i0 in range(0, n0, step):
            j0 = min(n0, i0 + step)
            pieces.append((t[i0:j0], s[i0:j0]))
    if em is None:
        return pieces
    for t_, s_ in pieces:
        em.dma("gpsimd", t_, s_, incs=(("cvx", 16),))
    return None


def p1_load_phase(em, p, g_early):
    d, sb = p.dram, p.sb
    em.nop("sync", waits=(("cvx", g_early),))
    em.dma("sync", sb["bqk"][:], d["bqk_col"], incs=(("l1x", 16),))
    em.dma("gpsimd", sb["p1_brow"][:], d["bv_row"], incs=(("l1x", 16),))
    em.dma("sync", sb["ones_bf"][:], d["ones_row_bf"], incs=(("l1x", 16),))
    xF = sb["xF"]
    for kt in range(16):
        em.dma("sync", xF[:, kt * NTOK:(kt + 1) * NTOK],
               d["xb"][:, kt * 128:(kt + 1) * 128],
               incs=(("l1x", 16),), transpose=True)


def _xf(p, kt, n0, n1):
    xF = p.sb["xF"]
    return xF[:, kt * NTOK + n0: kt * NTOK + n1]


def p1_v_phase(em, p):
    d, sb, ps = p.dram, p.sb, p.ps
    wv = sb["p1_wv"]
    ebuf = sb["p1_ebuf"]
    ones_bf = sb["ones_bf"]
    brow = sb["p1_brow"]
    for kt in range(16):
        em.dma("gpsimd", wv[:, kt * D:(kt + 1) * D], d["wv4"][kt],
               incs=(("l1v", 16),))
    loads_done = em.at("l1x")
    g_done = em.at("l1v")
    bank_g = [Guard() for _ in range(4)]
    ebuf_g = [Guard() for _ in range(2)]
    pidx = 0
    for tt in range(NTT):
        for pi in range(2):
            bp = (pidx % 2) * 2
            pidx += 1
            banks = [ps[f"pb32_{bp}"][:, 0:512], ps[f"pb32_{bp + 1}"][:, 0:512]]
            ns = (pi * 2, pi * 2 + 1)
            for kt in range(16):
                for j, n in enumerate(ns):
                    w0 = ((("l1x", loads_done), ("l1v", g_done))
                          if (tt == 0 and kt == 0) else ())
                    em.mm(banks[j], _xf(p, kt, tt * 128, (tt + 1) * 128),
                          wv[:, kt * D + n * 512: kt * D + (n + 1) * 512],
                          start=(kt == 0), stop=False,
                          waits=(bank_g[bp + j].w() + w0) if kt == 0 else ())
            for j, n in enumerate(ns):
                em.mm(banks[j], ones_bf[:], brow[:, n * 512:(n + 1) * 512],
                      start=False, stop=True, incs=(("pe", 1),))
            pe_v = em.at("pe")
            for j, n in enumerate(ns):
                slot = n % 2
                eb = ebuf[:, slot * 512:(slot + 1) * 512]
                em.act(eb, banks[j], AF.Copy, incs=(("ss", 1),),
                       waits=(("pe", pe_v),) + ebuf_g[slot].w())
                bank_g[bp + j].set("ss", em.at("ss"))
                em.dma("sync",
                       d["v_scr"][tt * 128:(tt + 1) * 128, n * 512:(n + 1) * 512],
                       eb, incs=((f"ze{slot}", 16),),
                       waits=(("ss", em.at("ss")),))
                ebuf_g[slot].set(f"ze{slot}", em.at(f"ze{slot}"))


def p1_qk_phase(em, p):
    d, sb, ps = p.dram, p.sb, p.ps
    wqk = sb["p1_wqk"]
    ebuf = sb["p1_ebuf"]
    bqk = sb["bqk"]
    bank_g = [Guard() for _ in range(4)]
    ebuf_g = [Guard() for _ in range(2)]
    wbuf_g = [Guard() for _ in range(2)]
    qk_pidx = [0]
    em.nop("scalar", waits=(("l1x", em.at("l1x")),))
    for mg in range(8):
        buf = mg % 2
        wsl = wqk[:, buf * 16 * 512:(buf + 1) * 16 * 512]
        for kt in range(16):
            em.dma("gpsimd", wsl[:, kt * 512:(kt + 1) * 512],
                   d["wqk4"][kt, :, mg * 512:(mg + 1) * 512],
                   incs=((f"lq{buf}", 16),),
                   waits=wbuf_g[buf].w() if kt == 0 else ())
        w_done = (f"lq{buf}", em.at(f"lq{buf}"))
        for t_, s_ in p.conv_pieces[mg * 3:(mg + 1) * 3]:
            em.dma("gpsimd", t_, s_, incs=(("cvp", 16),))
        for mi in range(4):
            m = mg * 4 + mi
            for pi in range(4):
                bp = (qk_pidx[0] % 2) * 2
                qk_pidx[0] += 1
                banks = [ps[f"pb32_{bp}"][:, 0:512],
                         ps[f"pb32_{bp + 1}"][:, 0:512]]
                ns = (pi * 2, pi * 2 + 1)
                for kt in range(16):
                    for j, n in enumerate(ns):
                        em.mm(banks[j],
                              wsl[:, kt * 512 + mi * 128: kt * 512 + (mi + 1) * 128],
                              _xf(p, kt, n * 512, (n + 1) * 512),
                              start=(kt == 0), stop=(kt == 15),
                              incs=((("pe", 1),) if kt == 15 else ()),
                              waits=(bank_g[bp + j].w() + (w_done,))
                              if kt == 0 else ())
                pe_qk = em.at("pe")
                for j, n in enumerate(ns):
                    slot = n % 2
                    eb = ebuf[:, slot * 512:(slot + 1) * 512]
                    em.act(eb, banks[j], AF.Copy, bias=bqk[:, m:m + 1],
                           incs=(("ss", 1),),
                           waits=(("pe", pe_qk),) + ebuf_g[slot].w())
                    bank_g[bp + j].set("ss", em.at("ss"))
                    tok0 = n * 512
                    b = tok0 // SEQ
                    for half in range(2):
                        c = (tok0 + half * 256 - b * SEQ) // CH
                        em.dma("sync",
                               d["qk_scr"][b, c, :, m * CH:(m + 1) * CH],
                               eb[:, half * 256:(half + 1) * 256],
                               incs=((f"ze{slot}", 16),),
                               waits=(("ss", em.at("ss")),))
                    ebuf_g[slot].set(f"ze{slot}", em.at(f"ze{slot}"))
        wbuf_g[buf].set("pe", em.at("pe"))


LAM_C = _consts()["lam_c"]


def p2_load_phase(em, p):
    d, sb = p.dram, p.sb
    woT = sb["p2_woT"]
    for kt in range(16):
        em.dma("gpsimd", woT[:, kt * D:(kt + 1) * D], d["woT3"][kt],
               incs=(("l2c", 16),))
    dm = sb["p2_dmask"]
    for h in range(NH):
        for t in range(2):
            em.dma("sync", dm[:, (h * 2 + t) * CH:(h * 2 + t + 1) * CH],
                   d["dmaskT"][h, t], incs=(("l2c", 16),))
    dq = sb["p2_dqb"]
    for h in range(NH):
        em.dma("sync", dq[:, h * CH:(h + 1) * CH], d["dqb"][h],
               incs=(("l2c", 16),))
    em.dma("sync", sb["p2_dk"][:], d["dkcol2"], incs=(("l2c", 16),))
    em.dma("sync", sb["ident_sb"][:], d["ident_bf"], incs=(("l2c", 16),))
    em.dma("gpsimd", sb["p2_brow"][:], d["bo_row"], incs=(("l2c", 16),))
    em.dma("sync", sb["eps_sb"][:], d["epscol"], incs=(("l2c", 16),))


def p2_phase(em, p):
    d, sb, ps = p.dram, p.sb, p.ps
    woT, dm, dq, dk = sb["p2_woT"], sb["p2_dmask"], sb["p2_dqb"], sb["p2_dk"]
    ident = sb["ident_sb"]
    ones_bf = sb["ones_bf"]
    brow_o = sb["p2_brow"]
    qkc, vc, xbc = sb["p2_qkc"], sb["p2_vc"], sb["p2_xbc"]
    S = sb["p2_S"]
    qd, kdT, scT = sb["p2_qd"], sb["p2_kdT"], sb["p2_scT"]
    attnF = sb["p2_attnF"]
    vln, y1b = sb["p2_vln"], sb["p2_y1b"]
    stats, junk = sb["p2_stats"], sb["p2_junk"]

    l2 = em.at("l2c")
    em.nop("tensor", waits=(("l2c", l2),))
    em.nop("vector", waits=(("l2c", l2),))
    em.nop("scalar", waits=(("l2c", l2),))

    bank_g = {k: Guard() for k in
              ("sc00", "sc01", "sc10", "sc11", "oT0", "oT1", "k0", "k1",
               "op0", "op1", "op2", "op3")}
    qkc_g = [Guard(), Guard()]
    vc_g = [Guard(), Guard()]
    xbc_g = [Guard(), Guard()]
    vln_g = [Guard(), Guard()]
    y1b_g = [Guard(), Guard()]
    junk_g = Guard()
    sbuf_g = {k: Guard() for k in ("qd", "kdT", "scT")}
    attnF_g = Guard()
    stats_g = [Guard(), Guard()]

    scb = [[ps["pb32_0"], ps["pb32_1"]], [ps["pb32_4"], ps["pb32_5"]]]
    oTb = [ps["pb32_2"], ps["pb32_3"]]
    pk = [ps["pb16_0"][:, 0:128], ps["pb16_1"][:, 0:128]]
    pop = [ps[f"pb32_{i}"][:, 0:512] for i in range(4)]

    QW = 32 * CH
    lds = {}

    def chunk_load(em, ci):
        b, c = divmod(ci, NCHUNK)
        sl = ci % 2
        sem = f"lk{sl}"
        em.dma("sync", qkc[:, sl * QW:(sl + 1) * QW], d["qk_scr"][b, c],
               incs=((sem, 16),), waits=qkc_g[sl].w())
        tok0 = b * SEQ + c * CH
        for t in range(2):
            em.dma("sync", vc[:, sl * 2 * D + t * D: sl * 2 * D + (t + 1) * D],
                   d["v_scr"][tok0 + t * 128: tok0 + (t + 1) * 128],
                   incs=((sem, 16),), waits=vc_g[sl].w() if t == 0 else ())
            em.dma("sync", xbc[:, sl * 2 * D + t * D: sl * 2 * D + (t + 1) * D],
                   d["xb"][tok0 + t * 128: tok0 + (t + 1) * 128],
                   incs=((sem, 16),), waits=xbc_g[sl].w() if t == 0 else ())
        lds[ci] = (sem, em.at(sem))

    chunk_load(em, 0)
    for b in range(B_PER_CORE):
        em.memset(S[:], 0.0, incs=(("sv", 1),),
                  waits=(("pe", em.at("pe")),))
        for c in range(NCHUNK):
            ci = b * NCHUNK + c
            sl = ci % 2
            if ci + 1 < 16:
                chunk_load(em, ci + 1)
            for t_, s_ in p.conv_pieces[24 + ci * 3: 24 + (ci + 1) * 3]:
                em.dma("gpsimd", t_, s_, incs=(("cvp", 16),))
            qof = sl * QW
            vof = sl * 2 * D
            tok0 = b * SEQ + c * CH
            ld = lds[ci]
            sv_S = em.at("sv")
            for bi, key in ((0, "sc00"), (1, "sc01"), (2, "oT0"), (3, "oT1")):
                bank_g[key].merge(bank_g[f"op{bi}"])

            for h in range(NH):
                par = h % 2
                psc = [scb[par][0][:, 0:256], scb[par][1][:, 0:256]]
                pS = scb[par][1][:, 256:384]
                poT = oTb[par][:, 0:256]
                kf = lambda t: qkc[:, qof + (16 + h) * CH + t * 128:
                                   qof + (16 + h) * CH + (t + 1) * 128]
                qf = qkc[:, qof + h * CH: qof + (h + 1) * CH]
                vt = lambda t: vc[:, vof + t * D + h * 128:
                                  vof + t * D + (h + 1) * 128]
                Sh = S[:, h * 128:(h + 1) * 128]

                for t in range(2):
                    em.tr(pk[t], kf(t), ident[:], incs=(("pe", 1),),
                          waits=(bank_g[f"k{t}"].w() + ((ld,)
                                 if h == 0 else ())))
                pe_k = em.at("pe")
                for t in range(2):
                    em.act(kdT[:, t * 128:(t + 1) * 128], pk[t], AF.Copy,
                           scale=dk[:, (h * 2 + t):(h * 2 + t + 1)],
                           incs=(("ss", 1),),
                           waits=(("pe", pe_k),) + sbuf_g["kdT"].w())
                    bank_g[f"k{t}"].set("ss", em.at("ss"))
                em.tt(qd[:], qf, dq[:, h * CH:(h + 1) * CH], ALU.mult,
                      incs=(("sv", 1),),
                      waits=sbuf_g["qd"].w()
                      + ((ld,) if h == 0 else ()))
                sv_kq = em.at("sv")

                for t in range(2):
                    em.mm(psc[t], kf(t), qf, start=True, stop=True,
                          incs=(("pe", 1),), waits=bank_g[f"sc{par}{t}"].w())
                pe_sc = em.at("pe")
                for t in range(2):
                    em.tt(scT[:, t * CH:(t + 1) * CH], psc[t],
                          dm[:, (h * 2 + t) * CH:(h * 2 + t + 1) * CH],
                          ALU.mult, incs=(("sv", 1),),
                          waits=(("pe", pe_sc),) + sbuf_g["scT"].w())
                    bank_g[f"sc{par}{t}"].set("sv", em.at("sv"))
                sv_sct = em.at("sv")

                first = True
                if c > 0:
                    em.mm(poT, Sh, qd[:], start=True, stop=False,
                          incs=(("pe", 1),),
                          waits=bank_g[f"oT{par}"].w() + (("sv", sv_kq),)
                          + (("sv", sv_S),))
                    first = False
                for t in range(2):
                    em.mm(poT, vt(t), scT[:, t * CH:(t + 1) * CH],
                          start=first, stop=(t == 1),
                          incs=(("pe", 1),),
                          waits=((("sv", sv_sct),)
                                 + (bank_g[f"oT{par}"].w() if first else ())))
                    first = False
                pe_oT = em.at("pe")

                for t in range(2):
                    em.mm(pS, kdT[:, t * 128:(t + 1) * 128], vt(t),
                          start=(t == 0), stop=(t == 1), incs=(("pe", 1),),
                          waits=((("sv", em.at("sv")), ("ss", em.at("ss"))))
                          if t == 0 else ())
                pe_S = em.at("pe")
                sbuf_g["kdT"].set("pe", pe_S)

                em.stt(Sh, Sh, float(LAM_C[h]), pS, ALU.mult, ALU.add,
                       incs=(("sv", 1),),
                       waits=(("pe", pe_S),))
                bank_g[f"sc{par}1"].set("sv", em.at("sv"))

                em.act(attnF[:, h * CH:(h + 1) * CH], poT, AF.Copy,
                       incs=(("ss", 1),),
                       waits=(("pe", pe_oT),) + attnF_g.w())
                bank_g[f"oT{par}"].set("ss", em.at("ss"))
                sbuf_g["scT"].set("pe", pe_oT)
                sbuf_g["qd"].set("pe", pe_oT)
            ss_attnF = em.at("ss")
            sv_S = em.at("sv")

            for bi, key in ((0, "sc00"), (1, "sc01"), (2, "oT0"), (3, "oT1")):
                bank_g[f"op{bi}"].merge(bank_g[key])
            for tti in range(2):
                for pi in range(2):
                    ns = (pi * 2, pi * 2 + 1)
                    bks = [pop[n] for n in ns]
                    for kt in range(16):
                        for j, n in enumerate(ns):
                            em.mm(bks[j],
                                  attnF[:, kt * CH + tti * 128: kt * CH + (tti + 1) * 128],
                                  woT[:, kt * D + n * 512: kt * D + (n + 1) * 512],
                                  start=(kt == 0), stop=False,
                                  waits=(bank_g[f"op{n}"].w()
                                         + (("ss", ss_attnF), ("sv", sv_S)))
                                  if kt == 0 else ())
                    for j, n in enumerate(ns):
                        em.mm(bks[j], ident[:],
                              xbc[:, vof + tti * D + n * 512:
                                  vof + tti * D + (n + 1) * 512],
                              start=False, stop=False)
                    for j, n in enumerate(ns):
                        em.mm(bks[j], ones_bf[:], brow_o[:, n * 512:(n + 1) * 512],
                              start=False, stop=True, incs=(("pe", 1),))
                    pe_op = em.at("pe")
                    for j, n in enumerate(ns):
                        em.act(vln[:, tti * D + n * 512: tti * D + (n + 1) * 512],
                               bks[j], AF.Copy,
                               accum_out=stats[:, tti * 4 + n: tti * 4 + n + 1],
                               incs=(("ss", 1),),
                               waits=(("pe", pe_op),) + vln_g[tti].w()
                               + (stats_g[tti].w() if n == 0 else ()))
                        bank_g[f"op{n}"].set("ss", em.at("ss"))
                ss_v = em.at("ss")
                sc_ = lambda j: stats[:, j:j + 1]
                em.tt(junk[:], vln[:, tti * D:(tti + 1) * D],
                      vln[:, tti * D:(tti + 1) * D], ALU.mult,
                      incs=(("sv", 1),), waits=(("ss", ss_v),) + junk_g.w())
                em.red(sc_(18 + tti), junk[:], ALU.add, incs=(("sv", 1),))
                junk_g.set("sv", em.at("sv"))
                em.red(sc_(16 + tti), stats[:, tti * 4: tti * 4 + 4], ALU.add,
                       incs=(("sv", 1),))
                sv_red = em.at("sv")
                stats_g[tti].set("sv", sv_red)
                em.act(sc_(20 + tti), sc_(16 + tti), AF.Copy,
                       scale=-1.0 / D, incs=(("ss", 1),),
                       waits=(("sv", sv_red),))
                em.act(sc_(22 + tti), sc_(16 + tti), AF.Copy, scale=1.0 / D,
                       incs=(("ss", 1),))
                em.act(sc_(24 + tti), sc_(18 + tti), AF.Copy, scale=1.0 / D,
                       incs=(("ss", 1),))
                ss_m = em.at("ss")
                em.stt(sc_(26 + tti), sc_(22 + tti), sc_(22 + tti),
                       sc_(24 + tti), ALU.mult, ALU.subtract,
                       incs=(("sv", 1),), waits=(("ss", ss_m),))
                em.act(sc_(28 + tti), sc_(26 + tti), AF.Sqrt, scale=-1.0,
                       bias=p.sb["eps_sb"][:, 0:1], incs=(("ss", 1),),
                       waits=(("sv", em.at("sv")),))
                em.recip(sc_(30 + tti), sc_(28 + tti), incs=(("sv", 1),),
                         waits=(("ss", em.at("ss")),))
                sv_rstd = em.at("sv")
                em.ts(vln[:, tti * D:(tti + 1) * D],
                      vln[:, tti * D:(tti + 1) * D],
                      sc_(20 + tti), ALU.add, s2=sc_(30 + tti), op1=ALU.mult,
                      incs=(("sv", 1),), waits=(("sv", sv_rstd),))
                em.cp(y1b[:, tti * D:(tti + 1) * D],
                      vln[:, tti * D:(tti + 1) * D], incs=(("sv", 1),),
                      waits=y1b_g[tti].w())
                sv_y1 = em.at("sv")
                trow = tok0 + tti * 128
                em.dma("gpsimd", d["y1_scr"][trow:trow + 128],
                       vln[:, tti * D:(tti + 1) * D],
                       incs=((f"zv{tti}", 16),), waits=(("sv", sv_y1),))
                vln_g[tti].set(f"zv{tti}", em.at(f"zv{tti}"))
                em.dma("gpsimd", d["y1b_scr"][trow:trow + 128],
                       y1b[:, tti * D:(tti + 1) * D],
                       incs=((f"zy{tti}", 16),), waits=(("sv", sv_y1),))
                y1b_g[tti].set(f"zy{tti}", em.at(f"zy{tti}"))
            pe_end = em.at("pe")
            qkc_g[sl].set("pe", pe_end)
            vc_g[sl].set("pe", pe_end)
            xbc_g[sl].set("pe", pe_end)
            attnF_g.set("pe", pe_end)


NCHUNKS_E = [h // 1024 for h in HID]


def p3a_phase(em, p, th):
    d, sb, ps = p.dram, p.sb, p.ps
    y1F, dw, gh = sb["y1F"], sb["p3_dw"], sb["gh"]
    rrw, rt = sb["p3_rrw"], sb["p3_rt"]
    ident32 = sb["p3_ident32"]
    iota = sb["p3_iota"]
    wmat, wpmat = sb["p3_wmat"], sb["p3_wpmat"]
    wTb, wpTb = sb["wTb"], sb["wpTb"]
    ones_bf = sb["ones_bf"]
    t0 = th * THALF

    for kt in range(16):
        em.dma("sync", y1F[:, kt * THALF:(kt + 1) * THALF],
               d["y1b_scr"][t0:t0 + THALF, kt * 128:(kt + 1) * 128],
               incs=(("l3a", 16),), transpose=True)
    for kt in range(16):
        em.dma("sync", dw[:, kt * LDIM:(kt + 1) * LDIM], d["down4b"][kt],
               incs=(("l3a", 16),))
    for kt in range(16):
        em.dma("gpsimd", rrw[:, kt * 8:(kt + 1) * 8], d["router3"][kt],
               incs=(("l3a", 16),))
    em.dma("sync", sb["p3_downb"][:], d["downb_col"], incs=(("l3a", 16),))
    em.dma("sync", ident32[:], d["ident_f32"], incs=(("l3a", 16),))
    em.dma("sync", iota[:], d["iota8"], incs=(("l3a", 16),))
    em.dma("sync", sb["p3_rb"][:], d["rb_row"], incs=(("l3a", 16),))
    em.dma("sync", sb["p3_nc"][:], d["negcost"], incs=(("l3a", 16),))
    ld = em.at("l3a")
    em.nop("scalar", waits=(("l3a", ld),))
    em.tt(sb["p3_rbf"][:], sb["p3_rb"][:], sb["p3_nc"][:], ALU.add,
          incs=(("sv", 1),), waits=(("l3a", ld),))
    em.cp(sb["p3_rbb"][:], sb["p3_rbf"][:], incs=(("sv", 1),),
          waits=(("sv", em.at("sv")),))
    sv_rb = em.at("sv")

    bank_g = [Guard() for _ in range(4)]
    for m in range(8):
        for pi in range(2):
            ns = (pi * 2, pi * 2 + 1)
            bks = [ps[f"pb32_{n}"][:, 0:512] for n in ns]
            for kt in range(16):
                for j, n in enumerate(ns):
                    em.mm(bks[j],
                          dw[:, kt * LDIM + m * 128: kt * LDIM + (m + 1) * 128],
                          y1F[:, kt * THALF + n * 512: kt * THALF + (n + 1) * 512],
                          start=(kt == 0), stop=(kt == 15),
                          incs=((("pe", 1),) if kt == 15 else ()),
                          waits=(bank_g[n].w() + (("l3a", ld),)) if kt == 0 else ())
            pe_d = em.at("pe")
            for j, n in enumerate(ns):
                em.act(gh[:, m * THALF + n * 512: m * THALF + (n + 1) * 512],
                       bks[j], AF.Gelu, bias=sb["p3_downb"][:, m:m + 1],
                       incs=(("ss", 1),), waits=(("pe", pe_d),))
                bank_g[n].set("ss", em.at("ss"))

    pr = ps["pb32_4"][:, 0:128]
    em.nop("tensor", waits=(("l3a", ld), ("sv", sv_rb)))
    for ttl in range(16):
        for kt in range(16):
            em.mm(pr[:, ttl * 8:(ttl + 1) * 8],
                  y1F[:, kt * THALF + ttl * 128: kt * THALF + (ttl + 1) * 128],
                  rrw[:, kt * 8:(kt + 1) * 8],
                  start=(kt == 0), stop=False)
        em.mm(pr[:, ttl * 8:(ttl + 1) * 8], ones_bf[:], sb["p3_rbb"][:],
              start=False, stop=True,
              incs=((("pe", 1),) if ttl == 15 else ()))
    pe_r = em.at("pe")

    def g16(i):
        return rt[:, 512 + i * 16: 512 + (i + 1) * 16]
    def v3(ap, w=8):
        return ap.rearrange("p (t e) -> p t e", e=w)
    def ttb(out, in0, small, op, rev=False, **kw):
        o3, a3 = v3(out), v3(in0)
        b3 = small.rearrange("p (t e) -> p t e", e=1)
        a3b, b3b = broadcast_tensor_aps(a3, b3)
        if rev:
            em.tt(o3, b3b, a3b, op, **kw)
        else:
            em.tt(o3, a3b, b3b, op, **kw)
    ex = rt[:, 0:128]
    pp = rt[:, 128:256]
    tcol = rt[:, 256:384]
    p2 = rt[:, 384:512]
    em.red(g16(0), v3(pr), ALU.max, incs=(("sv", 1),),
           waits=(("pe", pe_r), ("l3a", ld)))
    ttb(ex, pr, g16(0), ALU.subtract,
        incs=(("sv", 1),), waits=(("sv", em.at("sv")),))
    em.act(ex, ex, AF.Exp, incs=(("ss", 1),), waits=(("sv", em.at("sv")),))
    em.red(g16(1), v3(ex), ALU.add, incs=(("sv", 1),),
           waits=(("ss", em.at("ss")),))
    em.recip(g16(2), g16(1), incs=(("sv", 1),), waits=(("sv", em.at("sv")),))
    ttb(pp, ex, g16(2), ALU.mult,
        incs=(("sv", 1),), waits=(("sv", em.at("sv")),))
    em.red(g16(3), v3(pp), ALU.max, incs=(("sv", 1),),
           waits=(("sv", em.at("sv")),))
    ttb(tcol, pp, g16(3), ALU.is_equal,
        incs=(("sv", 1),), waits=(("sv", em.at("sv")),))
    em.stt(tcol, tcol, -NEG_IOTA_BIG, iota[:], ALU.mult, ALU.add,
           incs=(("sv", 1),), waits=(("sv", em.at("sv")),))
    em.red(g16(4), v3(tcol), ALU.min, incs=(("sv", 1),),
           waits=(("sv", em.at("sv")),))
    ttb(tcol, tcol, g16(4), ALU.is_equal,
        incs=(("sv", 1),), waits=(("sv", em.at("sv")),))
    em.stt(p2, tcol, -2.0, pp, ALU.mult, ALU.add, incs=(("sv", 1),),
           waits=(("sv", em.at("sv")),))
    em.red(g16(5), v3(p2), ALU.max, incs=(("sv", 1),),
           waits=(("sv", em.at("sv")),))
    ttb(p2, p2, g16(5), ALU.is_equal,
        incs=(("sv", 1),), waits=(("sv", em.at("sv")),))
    em.stt(p2, p2, -NEG_IOTA_BIG, iota[:], ALU.mult, ALU.add,
           incs=(("sv", 1),), waits=(("sv", em.at("sv")),))
    em.red(g16(6), v3(p2), ALU.min, incs=(("sv", 1),),
           waits=(("sv", em.at("sv")),))
    ttb(p2, p2, g16(6), ALU.is_equal,
        incs=(("sv", 1),), waits=(("sv", em.at("sv")),))
    em.tt(g16(7), g16(5), g16(3), ALU.subtract, incs=(("sv", 1),),
          waits=(("sv", em.at("sv")),))
    em.act(g16(8), g16(7), AF.Exp, incs=(("ss", 1),),
           waits=(("sv", em.at("sv")),))
    em.ts(g16(9), g16(8), 1.0, ALU.add, incs=(("sv", 1),),
          waits=(("ss", em.at("ss")),))
    em.recip(g16(10), g16(9), incs=(("sv", 1),), waits=(("sv", em.at("sv")),))
    em.tt(g16(11), g16(8), g16(10), ALU.mult, incs=(("sv", 1),),
          waits=(("sv", em.at("sv")),))
    em.tt(g16(12), g16(10), g16(11), ALU.add, incs=(("sv", 1),),
          waits=(("sv", em.at("sv")),))
    ttb(wmat[:], tcol, g16(10), ALU.mult,
        incs=(("sv", 1),), waits=(("sv", em.at("sv")),))
    ttb(p2, p2, g16(11), ALU.mult,
        incs=(("sv", 1),), waits=(("sv", em.at("sv")),))
    em.tt(wmat[:], wmat[:], p2, ALU.add, incs=(("sv", 1),),
          waits=(("sv", em.at("sv")),))
    ttb(wpmat[:], wmat[:], g16(12), ALU.subtract, rev=True,
        incs=(("sv", 1),), waits=(("sv", em.at("sv")),))
    em.vflush(p, wpmat[0:128, 0:8], incs=(("sv", 1),))
    em.vflush(p, wmat[0:128, 0:8], incs=(("sv", 1),))
    sv_w = em.at("sv")

    ptr32 = ps["pb32_5"]
    tr_g = Guard()
    for ttl in range(16):
        for wi, (src_m, dstb) in enumerate(((wmat, wTb), (wpmat, wpTb))):
            em.tr(ptr32[0:8, 0:128], src_m[:, ttl * 8:(ttl + 1) * 8],
                  ident32[:], incs=(("pe", 1),),
                  waits=(("sv", sv_w),) + tr_g.w())
            em.cp(dstb[0:8, ttl * 128:(ttl + 1) * 128], ptr32[0:8, 0:128],
                  incs=(("sv", 1),), waits=(("pe", em.at("pe")),))
            tr_g.set("sv", em.at("sv"))
            if ttl == 15:
                em.vflush(p, dstb[0:8, 0:8], incs=(("sv", 1),))


def p3b_phase(em, p, th):
    d, sb, ps = p.dram, p.sb, p.ps
    gh, acc, s1 = sb["gh"], sb["acc"], sb["p3_s1"]
    sh1, sh2 = sb["p3_sh1"], sb["p3_sh2"]
    for kt in range(8):
        em.dma("sync", sh1[:, kt * LDIM:(kt + 1) * LDIM], d["sh1_4b"][kt],
               incs=(("l3b", 16),))
        em.dma("sync", sh2[:, kt * LDIM:(kt + 1) * LDIM], d["sh2_3b"][kt],
               incs=(("l3b", 16),))
    em.dma("sync", sb["p3_sh1b"][:], d["sh1b_col"], incs=(("l3b", 16),))
    em.dma("sync", sb["p3_sh2b"][:], d["sh2b_col"], incs=(("l3b", 16),))
    ld = em.at("l3b")
    em.ts(sb["p3_sh2b01"][:], sb["p3_sh2b"][:], 0.1, ALU.mult,
          incs=(("sv", 1),), waits=(("l3b", ld),))
    sv_b = em.at("sv")
    em.nop("scalar", waits=(("l3b", ld), ("sv", sv_b)))
    bank_g = [Guard() for _ in range(4)]
    for m in range(8):
        for pi in range(2):
            ns = (pi * 2, pi * 2 + 1)
            bks = [ps[f"pb32_{n}"][:, 0:512] for n in ns]
            for kt in range(8):
                for j, n in enumerate(ns):
                    em.mm(bks[j],
                          sh1[:, kt * LDIM + m * 128: kt * LDIM + (m + 1) * 128],
                          gh[:, kt * THALF + n * 512: kt * THALF + (n + 1) * 512],
                          start=(kt == 0), stop=(kt == 7),
                          incs=((("pe", 1),) if kt == 7 else ()),
                          waits=(bank_g[n].w() + (("l3b", ld),)) if kt == 0 else ())
            pe_s = em.at("pe")
            for j, n in enumerate(ns):
                em.act(s1[:, m * THALF + n * 512: m * THALF + (n + 1) * 512],
                       bks[j], AF.Gelu, bias=sb["p3_sh1b"][:, m:m + 1],
                       incs=(("ss", 1),), waits=(("pe", pe_s),))
                bank_g[n].set("ss", em.at("ss"))
    ss_s1 = em.at("ss")
    for m in range(8):
        for pi in range(2):
            ns = (pi * 2, pi * 2 + 1)
            bks = [ps[f"pb32_{n}"][:, 0:512] for n in ns]
            for kt in range(8):
                for j, n in enumerate(ns):
                    em.mm(bks[j],
                          sh2[:, kt * LDIM + m * 128: kt * LDIM + (m + 1) * 128],
                          s1[:, kt * THALF + n * 512: kt * THALF + (n + 1) * 512],
                          start=(kt == 0), stop=(kt == 7),
                          incs=((("pe", 1),) if kt == 7 else ()),
                          waits=(bank_g[n].w() + (("ss", ss_s1),)
                                 + (("sv", sv_b),)) if kt == 0 else ())
            pe_s2 = em.at("pe")
            for j, n in enumerate(ns):
                em.act(acc[:, m * THALF + n * 512: m * THALF + (n + 1) * 512],
                       bks[j], AF.Copy, scale=0.1,
                       bias=sb["p3_sh2b01"][:, m:m + 1],
                       incs=(("ss", 1),), waits=(("pe", pe_s2),))
                bank_g[n].set("ss", em.at("ss"))


def p4_phase(em, p, th):
    d, sb, ps = p.dram, p.sb, p.ps
    gh, acc = sb["gh"], sb["acc"]
    z1g, fsb = sb["p4_z1g"], sb["p4_fsb"]
    w1b, w2b = sb["p4_w1b"], sb["p4_w2b"]
    wTb, wpTb = sb["wTb"], sb["wpTb"]
    wbcb = sb["p4_wbcb"]
    tmpb = sb["p4_tmp"]
    b1c, gb1 = sb["p4_b1c"], sb["p4_gb1"]
    b2c = sb["p4_b2c"]
    c_cols, c_mat = sb["p4_ccols"], sb["p4_cmat"]
    b2s = sb["p4_b2s"]
    sel8 = sb["p4_sel8"]
    ident = sb["ident_sb2"]

    em.dma("sync", b2c[:], d["b2cols"], incs=(("l4c", 16),))
    em.dma("gpsimd", sb["p4_b2s"][:], d["b2stack"], incs=(("l4c", 16),))
    em.dma("sync", sel8[:], d["sel8"], incs=(("l4c", 16),))
    em.dma("sync", ident[:], d["ident_bf"], incs=(("l4c", 16),))
    em.dma("sync", sb["p4_id32"][:], d["ident_f32"], incs=(("l4c", 16),))

    ld4 = em.at("l4c")
    em.nop("scalar", waits=(("l4c", ld4),))
    em.nop("vector", waits=(("l4c", ld4),))
    pz4 = [ps[f"pb32_{i}"][:, 0:512] for i in range(4)]
    pf4 = pz4
    pw = ps["pb32_4"][:, 0:512]
    pc = ps["pb32_5"][:, 0:32]
    zbank_g = [Guard() for _ in range(4)]
    fbank_g = zbank_g
    pw_g = Guard()
    pc_g = Guard()
    w1_g, w2_g, b1_g, gb1_g = Guard(), Guard(), Guard(), Guard()
    z1g_g, fsb_g = Guard(), Guard()
    tmp_g = [Guard(), Guard()]
    wbcb_g = Guard()
    ccols_g = Guard()

    for e in range(E):
        h = HID[e]
        nch = NCHUNKS_E[e]
        em.dma("sync", b1c[:, 0:h // 128], d[f"e{e}b1_col"],
               incs=(("lb1", 16),), waits=b1_g.w())
        em.act(gb1[:, 0:h // 128], b1c[:, 0:h // 128], AF.Gelu,
               incs=(("ss", 1),),
               waits=(("lb1", em.at("lb1")),) + gb1_g.w())
        ss_gb1 = em.at("ss")

        for n in range(4):
            em.mm(pw, sel8[:, e * 128:(e + 1) * 128],
                  wTb[:, n * 512:(n + 1) * 512], start=True, stop=True,
                  incs=(("pe", 1),), waits=pw_g.w() + (("l4c", ld4),))
            em.act(wbcb[:, n * 512:(n + 1) * 512], pw, AF.Copy,
                   incs=(("ss", 1),),
                   waits=(("pe", em.at("pe")),) + (wbcb_g.w() if n == 0 else ()))
            pw_g.set("ss", em.at("ss"))
        ss_wbc = em.at("ss")

        for c in range(nch):
            for kt in range(8):
                em.dma("sync", w1b[:, kt * LDIM:(kt + 1) * LDIM],
                       d[f"e{e}w1_4b"][kt, :, c * LDIM:(c + 1) * LDIM],
                       incs=(("lw1", 16),), waits=w1_g.w() if kt == 0 else ())
            w1_ld = em.at("lw1")
            for m in range(8):
                for pi in range(2):
                    bp = ((m * 2 + pi) % 2) * 2
                    ns = (pi * 2, pi * 2 + 1)
                    bks = [pz4[bp], pz4[bp + 1]]
                    for kt in range(8):
                        for j, n in enumerate(ns):
                            em.mm(bks[j],
                                  w1b[:, kt * LDIM + m * 128: kt * LDIM + (m + 1) * 128],
                                  gh[:, kt * THALF + n * 512: kt * THALF + (n + 1) * 512],
                                  start=(kt == 0), stop=(kt == 7),
                                  incs=((("pe", 1),) if kt == 7 else ()),
                                  waits=(zbank_g[bp + j].w() + (("lw1", w1_ld),)
                                         + (z1g_g.w() if (m == 0 and pi == 0)
                                            else ())) if kt == 0 else ())
                    pe_z = em.at("pe")
                    for j, n in enumerate(ns):
                        em.act(z1g[:, m * THALF + n * 512: m * THALF + (n + 1) * 512],
                               bks[j], AF.Gelu,
                               bias=b1c[:, c * 8 + m: c * 8 + m + 1],
                               incs=(("ss", 1),), waits=(("pe", pe_z),))
                        zbank_g[bp + j].set("ss", em.at("ss"))
            w1_g.set("pe", em.at("pe"))
            ss_z1 = em.at("ss")

            for kt in range(8):
                em.dma("sync", w2b[:, kt * LDIM:(kt + 1) * LDIM],
                       d[f"e{e}w2_3b"][c * 8 + kt],
                       incs=(("lw2", 16),), waits=w2_g.w() if kt == 0 else ())
            w2_ld = em.at("lw2")
            for m2 in range(8):
                for pi in range(2):
                    fbp = ((m2 * 2 + pi) % 2) * 2
                    ns = (pi * 2, pi * 2 + 1)
                    bksf = [pf4[fbp], pf4[fbp + 1]]
                    for kt in range(8):
                        for j, n in enumerate(ns):
                            em.mm(bksf[j],
                                  w2b[:, kt * LDIM + m2 * 128: kt * LDIM + (m2 + 1) * 128],
                                  z1g[:, kt * THALF + n * 512: kt * THALF + (n + 1) * 512],
                                  start=(kt == 0), stop=(kt == 7),
                                  incs=((("pe", 1),) if kt == 7 else ()),
                                  waits=(fbank_g[fbp + j].w() + (("lw2", w2_ld),)
                                         + (("ss", ss_z1),)) if kt == 0 else ())
                        if pi == 0:
                            em.mm(pc[:, c * 8 + m2: c * 8 + m2 + 1],
                                  w2b[:, kt * LDIM + m2 * 128: kt * LDIM + (m2 + 1) * 128],
                                  gb1[:, c * 8 + kt: c * 8 + kt + 1],
                                  start=(kt == 0), stop=(kt == 7),
                                  incs=((("pe", 1),) if (kt == 7 and m2 == 7)
                                        else ()),
                                  waits=(("ss", ss_gb1),) + (pc_g.w()
                                         if (c == 0 and kt == 0 and m2 == 0)
                                         else ()))
                    pe_f = em.at("pe")
                    for j, n in enumerate(ns):
                        bank = bksf[j]
                        slot = (m2 * 4 + n) % 2
                        tslot = tmpb[:, slot * 512:(slot + 1) * 512]
                        dst = fsb[:, m2 * THALF + n * 512:
                                  m2 * THALF + (n + 1) * 512]
                        asl = acc[:, m2 * THALF + n * 512:
                                  m2 * THALF + (n + 1) * 512]
                        if nch == 1:
                            em.act(tslot, bank, AF.Copy,
                                   bias=b2c[:, e * 8 + m2: e * 8 + m2 + 1],
                                   incs=(("ss", 1),),
                                   waits=(("pe", pe_f),) + tmp_g[slot].w())
                            fbank_g[fbp + j].set("ss", em.at("ss"))
                            em.tt(tslot, tslot,
                                  wbcb[:, n * 512:(n + 1) * 512], ALU.mult,
                                  incs=(("sv", 1),),
                                  waits=(("ss", em.at("ss")), ("ss", ss_wbc)))
                            em.tt(asl, asl, tslot, ALU.add, incs=(("sv", 1),))
                            tmp_g[slot].set("sv", em.at("sv"))
                        elif c == 0:
                            em.act(dst, bank, AF.Copy, incs=(("ss", 1),),
                                   waits=(("pe", pe_f),)
                                   + (fsb_g.w() if (m2 == 0 and n == 0) else ()))
                            fbank_g[fbp + j].set("ss", em.at("ss"))
                        elif c < nch - 1:
                            em.act(tslot, bank, AF.Copy, incs=(("ss", 1),),
                                   waits=(("pe", pe_f),) + tmp_g[slot].w())
                            fbank_g[fbp + j].set("ss", em.at("ss"))
                            em.tt(dst, dst, tslot, ALU.add, incs=(("sp", 1),),
                                  waits=(("ss", em.at("ss")),), eng="gpsimd")
                            tmp_g[slot].set("sp", em.at("sp"))
                        else:
                            em.act(tslot, bank, AF.Copy,
                                   bias=b2c[:, e * 8 + m2: e * 8 + m2 + 1],
                                   incs=(("ss", 1),),
                                   waits=(("pe", pe_f),) + tmp_g[slot].w())
                            fbank_g[fbp + j].set("ss", em.at("ss"))
                            em.tt(tslot, tslot, dst, ALU.add, incs=(("sv", 1),),
                                  waits=(("ss", em.at("ss")), ("sp", em.at("sp"))))
                            em.tt(tslot, tslot,
                                  wbcb[:, n * 512:(n + 1) * 512], ALU.mult,
                                  incs=(("sv", 1),), waits=(("ss", ss_wbc),))
                            em.tt(asl, asl, tslot, ALU.add, incs=(("sv", 1),))
                            tmp_g[slot].set("sv", em.at("sv"))
            w2_g.set("pe", em.at("pe"))
            z1g_g.set("pe", em.at("pe"))
        fsb_g.set("sv", em.at("sv"))
        wbcb_g.set("sv", em.at("sv"))
        b1_g.set("ss", em.at("ss"))
        gb1_g.set("pe", em.at("pe"))
        em.red(c_cols[:, e * 8:(e + 1) * 8],
               pc[:, 0:nch * 8].rearrange("p (c m) -> p m c", m=8),
               ALU.add, incs=(("sv", 1),),
               waits=(("pe", em.at("pe")),))
        em.vflush(p, c_cols[:, e * 8:(e + 1) * 8], incs=(("sv", 1),))
        pc_g.set("sv", em.at("sv"))

    sv_cc = em.at("sv")
    trg = Guard()
    for m2 in range(8):
        cin = c_cols[:, :].rearrange("p (e m) -> p e m", m=8)[:, :, m2]
        em.tr(pw[0:8, 0:128], cin, sb["p4_id32"][:], incs=(("pe", 1),),
              waits=(("sv", sv_cc), ("l4c", ld4)) + trg.w() + pw_g.w())
        em.cp(c_mat[0:8, m2 * 128:(m2 + 1) * 128], pw[0:8, 0:128],
              incs=(("sv", 1),), waits=(("pe", em.at("pe")),))
        trg.set("sv", em.at("sv"))
        pw_g.set("sv", em.at("sv"))
    em.vflush(p, c_mat[0:8, 0:8], incs=(("sv", 1),))
    sv_cm = em.at("sv")
    cb_g = [Guard(), Guard()]
    for m2 in range(8):
        for n in range(4):
            cb = (m2 * 4 + n) % 2
            bank = pz4[cb]
            em.mm(bank, c_mat[:, m2 * 128:(m2 + 1) * 128],
                  wpTb[:, n * 512:(n + 1) * 512], start=True, stop=False,
                  waits=(("sv", sv_cm), ("l4c", ld4))
                  + cb_g[cb].w() + fbank_g[cb].w())
            em.mm(bank, b2s[:, m2 * 128:(m2 + 1) * 128],
                  wpTb[:, n * 512:(n + 1) * 512], start=False, stop=True,
                  incs=(("pe", 1),))
            em.tt(acc[:, m2 * THALF + n * 512: m2 * THALF + (n + 1) * 512],
                  acc[:, m2 * THALF + n * 512: m2 * THALF + (n + 1) * 512],
                  bank, ALU.add, incs=(("sv", 1),),
                  waits=(("pe", em.at("pe")),))
            cb_g[cb].set("sv", em.at("sv"))
            fbank_g[cb].set("sv", em.at("sv"))


def p5_phase(em, p, th, debug=False):
    d, sb, ps = p.dram, p.sb, p.ps
    acc = sb["acc"]
    upw, corew = sb["p5_upw"], sb["p5_corew"]
    gy1F = sb["p5_gy1F"]
    y1bt = sb["p5_y1bt"]
    vln, stats = sb["p5_vln"], sb["p5_stats"]
    ident = sb["ident_sb2"]
    ones_bf = sb["ones_bf"]
    ubcb = sb["ubcb"]
    t0 = th * THALF

    for kt in range(8):
        em.dma("sync", upw[:, kt * D:(kt + 1) * D], d["up3b"][kt],
               incs=(("l5", 16),))
    for kt in range(16):
        em.dma("sync", corew[:, kt * D:(kt + 1) * D], d["core3b"][kt],
               incs=(("l5", 16),))
    QT = 1024
    pop = [ps[f"pb32_{i}"][:, 0:512] for i in range(4)]
    bank_g = [Guard() for _ in range(4)]
    y1bt_g = [Guard(), Guard()]
    vln_g = [Guard(), Guard()]
    stats_g = [Guard(), Guard()]
    gy1_g = Guard()

    for qa in range(2):
      tq = t0 + qa * QT
      for kt in range(16):
          em.dma("sync", gy1F[:, kt * QT:(kt + 1) * QT],
                 d["y1b_scr"][tq:tq + QT, kt * 128:(kt + 1) * 128],
                 incs=(("l5", 16),), transpose=True,
                 waits=gy1_g.w() if kt == 0 else ())
      ld0 = em.at("l5")
      for kt in range(16):
          em.act(gy1F[:, kt * QT:(kt + 1) * QT],
                 gy1F[:, kt * QT:(kt + 1) * QT], AF.Gelu,
                 incs=(("ss", 1),), waits=((("l5", ld0),) if kt == 0 else ()))
      ss_gy1 = em.at("ss")

      for ttl in range(qa * 8, qa * 8 + 8):
        tql = ttl - qa * 8
        sl = ttl % 2
        y1s = y1bt[:, sl * D:(sl + 1) * D]
        trow = t0 + ttl * 128
        em.dma("sync", y1s, d["y1b_scr"][trow:trow + 128],
               incs=((f"lt{sl}", 16),), waits=y1bt_g[sl].w())
        ld = (f"lt{sl}", em.at(f"lt{sl}"))
        for pi in range(2):
            ns = (pi * 2, pi * 2 + 1)
            bks = [pop[n] for n in ns]
            for kt in range(8):
                for j, n in enumerate(ns):
                    em.mm(bks[j],
                          acc[:, kt * THALF + ttl * 128: kt * THALF + (ttl + 1) * 128],
                          upw[:, kt * D + n * 512: kt * D + (n + 1) * 512],
                          start=(kt == 0), stop=False,
                          waits=(bank_g[n].w() + (ld, ("ss", ss_gy1)))
                          if kt == 0 else ())
            for kt in range(16):
                for j, n in enumerate(ns):
                    em.mm(bks[j],
                          gy1F[:, kt * QT + tql * 128: kt * QT + (tql + 1) * 128],
                          corew[:, kt * D + n * 512: kt * D + (n + 1) * 512],
                          start=False, stop=False)
            for j, n in enumerate(ns):
                em.mm(bks[j], ident[:], y1s[:, n * 512:(n + 1) * 512],
                      start=False, stop=False)
            for j, n in enumerate(ns):
                em.mm(bks[j], ones_bf[:], ubcb[:, n * 512:(n + 1) * 512],
                      start=False, stop=True, incs=(("pe", 1),))
            pe_op = em.at("pe")
            for j, n in enumerate(ns):
                em.act(vln[:, sl * D + n * 512: sl * D + (n + 1) * 512],
                       bks[j], AF.Copy,
                       accum_out=stats[:, sl * 16 + n: sl * 16 + n + 1],
                       incs=(("ss", 1),),
                       waits=(("pe", pe_op),) + vln_g[sl].w()
                       + (stats_g[sl].w() if n == 0 else ()))
                em.act(sb["p5_junk"][:], bks[j], AF.Square,
                       accum_out=stats[:, sl * 16 + 4 + n: sl * 16 + 5 + n],
                       incs=(("ss", 1),))
                bank_g[n].set("ss", em.at("ss"))
        y1bt_g[sl].set("pe", em.at("pe"))
        ss_v = em.at("ss")
        sc_ = lambda j: stats[:, sl * 16 + j: sl * 16 + j + 1]
        em.red(sc_(8), stats[:, sl * 16: sl * 16 + 4], ALU.add,
               incs=(("sv", 1),), waits=(("ss", ss_v),))
        em.red(sc_(9), stats[:, sl * 16 + 4: sl * 16 + 8], ALU.add,
               incs=(("sv", 1),))
        sv_red = em.at("sv")
        stats_g[sl].set("sv", sv_red)
        em.act(sc_(10), sc_(8), AF.Copy, scale=-1.0 / D, incs=(("ss", 1),),
               waits=(("sv", sv_red),))
        em.act(sc_(11), sc_(8), AF.Copy, scale=1.0 / D, incs=(("ss", 1),))
        em.act(sc_(12), sc_(9), AF.Copy, scale=1.0 / D, incs=(("ss", 1),))
        ss_m = em.at("ss")
        em.stt(sc_(13), sc_(11), sc_(11), sc_(12), ALU.mult, ALU.subtract,
               incs=(("sv", 1),), waits=(("ss", ss_m),))
        em.act(sc_(14), sc_(13), AF.Sqrt, scale=-1.0,
               bias=p.sb["eps_sb"][:, 0:1], incs=(("ss", 1),),
               waits=(("sv", em.at("sv")),))
        em.recip(sc_(15), sc_(14), incs=(("sv", 1),),
                 waits=(("ss", em.at("ss")),))
        sv_rstd = em.at("sv")
        em.ts(vln[:, sl * D:(sl + 1) * D], vln[:, sl * D:(sl + 1) * D],
              sc_(10), ALU.add, s2=sc_(15), op1=ALU.mult,
              incs=(("sv", 1),), waits=(("sv", sv_rstd),))
        sv_y2 = em.at("sv")
        if debug:
            em.dma("gpsimd", d["dbg_s5"][th * 16 + ttl], stats[:],
                   incs=(("zd", 16),), waits=(("sv", sv_y2),))
            stats_g[sl].set("zd", em.at("zd"))
        em.dma("gpsimd", d["out"][trow:trow + 128],
               vln[:, sl * D:(sl + 1) * D], incs=((f"zo{sl}", 16),),
               waits=(("sv", sv_y2),))
        vln_g[sl].set(f"zo{sl}", em.at(f"zo{sl}"))
      gy1_g.set("pe", em.at("pe"))


DEBUG = False


def _emit_block(nc, p, phase_fn, snap, engines=ENGINES):
    ends = []
    with nc.Block() as bk:
        for engname in engines:
            def body(eng, engname=engname):
                em = Em(engname, eng, p.sems, snap)
                phase_fn(em)
                ends.append(dict(em.c))
            getattr(bk, engname)(body)
    for e2 in ends[1:]:
        assert e2 == ends[0], f"sem count divergence: {ends[0]} vs {e2}"
    return ends[0]


def _barrier(nc, p, snap):
    def fn(em):
        targets = tuple((s, em.at(s)) for s in SEMS if em.at(s) > 0)
        for eng in ENGINES:
            em.nop(eng, waits=targets)
    return _emit_block(nc, p, fn, snap)


def build_nc(debug=False):
    nc = bass.Bass("TRN2", target_bir_lowering=False, debug=False)
    p = P()
    _declare_drams(nc, p, debug=debug)
    stack = ExitStack()
    with stack:
        for s in SEMS:
            p.sems[s] = stack.enter_context(nc.semaphore(s))
        for i in range(6):
            p.ps[f"pb32_{i}"] = stack.enter_context(
                nc.psum_tensor(f"pb32_{i}", [128, 512], F32))
        for i in range(2):
            p.ps[f"pb16_{i}"] = stack.enter_context(
                nc.psum_tensor(f"pb16_{i}", [128, 1024], BF16))
        p.sb["ones_bf"] = stack.enter_context(
            nc.sbuf_tensor("ones_bf", [1, 128], BF16))
        p.sb["bqk"] = stack.enter_context(nc.sbuf_tensor("bqk", [128, 32], F32))
        p.sb["eps_sb"] = stack.enter_context(
            nc.sbuf_tensor("eps_sb", [128, 1], F32))
        p.sb["ubcb"] = stack.enter_context(
            nc.sbuf_tensor("ubcb", [1, D], BF16))
        p.sb["vscr"] = stack.enter_context(
            nc.sbuf_tensor("vscr", [128, 16], F32))

        p.conv_pieces = conv_phase(None, p, False)
        assert len(p.conv_pieces) <= 24 + 48, len(p.conv_pieces)
        snap = _emit_block(nc, p, lambda em: conv_phase(em, p, True), {})
        g_early = snap["cvx"]

        with ExitStack() as s0:
            p.sb["i_ub"] = s0.enter_context(nc.sbuf_tensor("i_ub", [1, D], F32))
            p.sb["i_cb"] = s0.enter_context(nc.sbuf_tensor("i_cb", [1, D], F32))

            def ubcb_init(em):
                em.dma("sync", p.sb["i_ub"][:], p.dram["upb_row"],
                       incs=(("li", 16),))
                em.dma("sync", p.sb["i_cb"][:], p.dram["coreb_row"],
                       incs=(("li", 16),))
                em.tt(p.sb["i_ub"][:], p.sb["i_ub"][:], p.sb["i_cb"][:],
                      ALU.add, incs=(("sv", 1),), waits=(("li", em.at("li")),))
                em.cp(p.sb["ubcb"][:], p.sb["i_ub"][:], incs=(("sv", 1),),
                      waits=(("sv", em.at("sv")),))
            snap = _emit_block(nc, p, ubcb_init, snap)
        snap = _barrier(nc, p, snap)

        with ExitStack() as s1:
            p.sb["xF"] = s1.enter_context(
                nc.sbuf_tensor("xF", [128, 16 * NTOK], BF16))
            p.sb["p1_ebuf"] = s1.enter_context(
                nc.sbuf_tensor("p1_ebuf", [128, 2 * 512], BF16))
            p.sb["p1_brow"] = s1.enter_context(
                nc.sbuf_tensor("p1_brow", [1, D], BF16))
            snap = _emit_block(
                nc, p, lambda em: p1_load_phase(em, p, g_early), snap)
            snap = _barrier(nc, p, snap)
            with ExitStack() as s1v:
                p.sb["p1_wv"] = s1v.enter_context(
                    nc.sbuf_tensor("p1_wv", [128, 16 * D], BF16))
                snap = _emit_block(nc, p, lambda em: p1_v_phase(em, p), snap)
            snap = _barrier(nc, p, snap)
            with ExitStack() as s1q:
                p.sb["p1_wqk"] = s1q.enter_context(
                    nc.sbuf_tensor("p1_wqk", [128, 2 * 16 * 512], BF16))
                snap = _emit_block(nc, p, lambda em: p1_qk_phase(em, p), snap)

        snap = _barrier(nc, p, snap)
        with ExitStack() as s2:
            sb = p.sb
            t = nc.sbuf_tensor
            sb["p2_woT"] = s2.enter_context(t("p2_woT", [128, 16 * D], BF16))
            sb["p2_dmask"] = s2.enter_context(t("p2_dmask", [128, 32 * CH], BF16))
            sb["p2_dqb"] = s2.enter_context(t("p2_dqb", [128, 16 * CH], BF16))
            sb["p2_dk"] = s2.enter_context(t("p2_dk", [128, 32], F32))
            sb["ident_sb"] = s2.enter_context(t("ident_sb", [128, 128], BF16))
            sb["p2_brow"] = s2.enter_context(t("p2_brow", [1, D], BF16))
            sb["p2_qkc"] = s2.enter_context(t("p2_qkc", [128, 2 * 32 * CH], BF16))
            sb["p2_vc"] = s2.enter_context(t("p2_vc", [128, 4 * D], BF16))
            sb["p2_xbc"] = s2.enter_context(t("p2_xbc", [128, 4 * D], BF16))
            sb["p2_S"] = s2.enter_context(t("p2_S", [128, 16 * 128], BF16))
            sb["p2_qd"] = s2.enter_context(t("p2_qd", [128, CH], BF16))
            sb["p2_kdT"] = s2.enter_context(t("p2_kdT", [128, 2 * 128], BF16))
            sb["p2_scT"] = s2.enter_context(t("p2_scT", [128, 2 * CH], BF16))
            sb["p2_attnF"] = s2.enter_context(t("p2_attnF", [128, 16 * CH], BF16))
            sb["p2_vln"] = s2.enter_context(t("p2_vln", [128, 2 * D], F32))
            sb["p2_y1b"] = s2.enter_context(t("p2_y1b", [128, 2 * D], BF16))
            sb["p2_stats"] = s2.enter_context(t("p2_stats", [128, 32], F32))
            sb["p2_junk"] = s2.enter_context(t("p2_junk", [128, D], F32))
            snap = _emit_block(nc, p, lambda em: p2_load_phase(em, p), snap)
            snap = _emit_block(nc, p, lambda em: p2_phase(em, p), snap)

        g_late = snap["cvp"]

        for th in range(2):
            snap = _barrier(nc, p, snap)
            with ExitStack() as sh:
                t = nc.sbuf_tensor
                sb = p.sb
                sb["acc"] = sh.enter_context(t(f"acc{th}", [128, 8 * THALF], BF16))
                sb["ident_sb2"] = sh.enter_context(t(f"ident2_{th}", [128, 128], BF16))
                sgh = ExitStack()
                sb["gh"] = sgh.enter_context(t(f"gh{th}", [128, 8 * THALF], BF16))
                sb["wTb"] = sgh.enter_context(t(f"wTb{th}", [8, 2048], BF16))
                sb["wpTb"] = sgh.enter_context(t(f"wpTb{th}", [8, 2048], BF16))
                with ExitStack() as s3:
                    sb["y1F"] = s3.enter_context(t(f"y1F_{th}", [128, 16 * THALF], BF16))
                    sb["p3_dw"] = s3.enter_context(t(f"p3_dw_{th}", [128, 16 * LDIM], BF16))
                    sb["p3_rrw"] = s3.enter_context(t(f"p3_rrw_{th}", [128, 16 * 8], BF16))
                    sb["p3_rt"] = s3.enter_context(t(f"p3_rt_{th}", [128, 512 + 13 * 16], F32))
                    sb["p3_ident32"] = s3.enter_context(t(f"p3_ident32_{th}", [128, 128], F32))
                    sb["p3_iota"] = s3.enter_context(t(f"p3_iota_{th}", [128, 128], F32))
                    sb["p3_wmat"] = s3.enter_context(t(f"p3_wmat_{th}", [128, 128], F32))
                    sb["p3_wpmat"] = s3.enter_context(t(f"p3_wpmat_{th}", [128, 128], F32))
                    sb["p3_downb"] = s3.enter_context(t(f"p3_downb_{th}", [128, 8], F32))
                    sb["p3_rb"] = s3.enter_context(t(f"p3_rb_{th}", [1, 8], F32))
                    sb["p3_nc"] = s3.enter_context(t(f"p3_nc_{th}", [1, 8], F32))
                    sb["p3_rbf"] = s3.enter_context(t(f"p3_rbf_{th}", [1, 8], F32))
                    sb["p3_rbb"] = s3.enter_context(t(f"p3_rbb_{th}", [1, 8], BF16))
                    if th == 0:
                        def wconv(em):
                            for eng in ENGINES:
                                em.nop(eng, waits=(("cvp", g_late),))
                        snap = _emit_block(nc, p, wconv, snap)
                    snap = _emit_block(nc, p, lambda em: p3a_phase(em, p, th), snap)
                    if debug:
                        def dbg3a(em):
                            dd = p.dram
                            em.dma("gpsimd", dd["dbg_gh"][th], p.sb["gh"][:],
                                   incs=(("zd", 16),), waits=tuple(
                                       (s, em.at(s)) for s in ("ss", "sv", "pe")))
                            em.dma("gpsimd", dd["dbg_w"][th], p.sb["p3_wmat"][:],
                                   incs=(("zd", 16),))
                            em.dma("gpsimd", dd["dbg_wp"][th], p.sb["p3_wpmat"][:],
                                   incs=(("zd", 16),))
                            em.dma("gpsimd", dd["dbg_wT"][th], p.sb["wTb"][:],
                                   incs=(("zd", 16),))
                            em.nop("gpsimd", waits=(("zd", em.at("zd")),))
                        snap = _emit_block(nc, p, dbg3a, snap, engines=("gpsimd",))
                snap = _barrier(nc, p, snap)
                with ExitStack() as s3b:
                    sb["p3_s1"] = s3b.enter_context(t(f"p3_s1_{th}", [128, 8 * THALF], BF16))
                    sb["p3_sh1"] = s3b.enter_context(t(f"p3_sh1_{th}", [128, 8 * LDIM], BF16))
                    sb["p3_sh2"] = s3b.enter_context(t(f"p3_sh2_{th}", [128, 8 * LDIM], BF16))
                    sb["p3_sh1b"] = s3b.enter_context(t(f"p3_sh1b_{th}", [128, 8], F32))
                    sb["p3_sh2b"] = s3b.enter_context(t(f"p3_sh2b_{th}", [128, 8], F32))
                    sb["p3_sh2b01"] = s3b.enter_context(t(f"p3_sh2b01_{th}", [128, 8], F32))
                    snap = _emit_block(nc, p, lambda em: p3b_phase(em, p, th), snap)
                    if debug:
                        def dbg3b(em):
                            em.dma("gpsimd", p.dram["dbg_accsh"][th], p.sb["acc"][:],
                                   incs=(("zd", 16),), waits=tuple(
                                       (s, em.at(s)) for s in ("ss", "sv", "pe")))
                            em.nop("gpsimd", waits=(("zd", em.at("zd")),))
                        snap = _emit_block(nc, p, dbg3b, snap, engines=("gpsimd",))
                snap = _barrier(nc, p, snap)
                with ExitStack() as s4:
                    sb["p4_z1g"] = s4.enter_context(t(f"p4_z1g_{th}", [128, 8 * THALF], BF16))
                    sb["p4_fsb"] = s4.enter_context(t(f"p4_fsb_{th}", [128, 8 * THALF], BF16))
                    sb["p4_w1b"] = s4.enter_context(t(f"p4_w1b_{th}", [128, 8 * LDIM], BF16))
                    sb["p4_w2b"] = s4.enter_context(t(f"p4_w2b_{th}", [128, 8 * LDIM], BF16))
                    sb["p4_wbcb"] = s4.enter_context(t(f"p4_wbcb_{th}", [128, 4 * 512], BF16))
                    sb["p4_tmp"] = s4.enter_context(t(f"p4_tmp_{th}", [128, 2 * 512], BF16))
                    sb["p4_b1c"] = s4.enter_context(t(f"p4_b1c_{th}", [128, 32], F32))
                    sb["p4_gb1"] = s4.enter_context(t(f"p4_gb1_{th}", [128, 32], BF16))
                    sb["p4_b2c"] = s4.enter_context(t(f"p4_b2c_{th}", [128, 64], F32))
                    sb["p4_ccols"] = s4.enter_context(t(f"p4_ccols_{th}", [128, 64], F32))
                    sb["p4_id32"] = s4.enter_context(t(f"p4_id32_{th}", [128, 128], F32))
                    sb["p4_b2s"] = s4.enter_context(t(f"p4_b2s_{th}", [8, LDIM], BF16))
                    sb["p4_cmat"] = s4.enter_context(t(f"p4_cmat_{th}", [8, 1024], BF16))
                    sb["p4_sel8"] = s4.enter_context(t(f"p4_sel8_{th}", [8, 1024], BF16))
                    snap = _emit_block(nc, p, lambda em: p4_phase(em, p, th), snap)
                    if debug:
                        def dbg4(em):
                            em.dma("gpsimd", p.dram["dbg_acc"][th], p.sb["acc"][:],
                                   incs=(("zd", 16),), waits=tuple(
                                       (s, em.at(s)) for s in ("ss", "sv", "pe", "sp")))
                            em.dma("gpsimd", p.dram["dbg_cmat"][th], p.sb["p4_cmat"][:],
                                   incs=(("zd", 16),))
                            em.nop("gpsimd", waits=(("zd", em.at("zd")),))
                        snap = _emit_block(nc, p, dbg4, snap, engines=("gpsimd",))
                sgh.close()
                snap = _barrier(nc, p, snap)
                if True:
                    with ExitStack() as s5:
                        sb["p5_upw"] = s5.enter_context(t(f"p5_upw_{th}", [128, 8 * D], BF16))
                        sb["p5_corew"] = s5.enter_context(t(f"p5_corew_{th}", [128, 16 * D], BF16))
                        sb["p5_y1bt"] = s5.enter_context(t(f"p5_y1bt_{th}", [128, 2 * D], BF16))
                        sb["p5_gy1F"] = s5.enter_context(t(f"p5_gy1F_{th}", [128, 16 * 1024], BF16))
                        sb["p5_vln"] = s5.enter_context(t(f"p5_vln_{th}", [128, 2 * D], F32))
                        sb["p5_junk"] = s5.enter_context(t(f"p5_junk_{th}", [128, 512], F32))
                        sb["p5_stats"] = s5.enter_context(t(f"p5_stats_{th}", [128, 32], F32))
                        snap = _emit_block(
                            nc, p, lambda em: p5_phase(em, p, th, debug=debug),
                            snap)

        def fin(em):
            em.nop("gpsimd", waits=tuple(
                (s, em.at(s)) for s in
                ("ze0", "ze1", "zv0", "zv1", "zy0", "zy1", "zo0", "zo1", "zd")
                if em.at(s) > 0))
        snap = _emit_block(nc, p, fin, snap, engines=("gpsimd",))
    return nc


def _prep_inputs(inputs):
    c = _consts()
    g = {}
    wqkv = inputs["L0_attn_w_qkv"]
    g["wqk4"] = _rhs_ktiles(np.ascontiguousarray(wqkv[:2 * D].T))
    g["wv4"] = _rhs_ktiles(np.ascontiguousarray(wqkv[2 * D:].T))
    bqkv = inputs["L0_attn_b_qkv"]
    g["bqk_col"] = _bias_col(bqkv[:2 * D])
    g["bv_row"] = np.ascontiguousarray(bqkv[2 * D:].reshape(1, D))
    g["woT3"] = _rhs_ktiles(np.ascontiguousarray(inputs["L0_attn_w_out"].T))
    g["bo_row"] = np.ascontiguousarray(inputs["L0_attn_b_out"].reshape(1, D))
    g["down4"] = _rhs_ktiles(inputs["L0_ffn_down_w"])
    g["downb_col"] = _bias_col(inputs["L0_ffn_down_b"])
    g["router3"] = _rhs_ktiles(inputs["L0_ffn_router_w"])
    g["rb_row"] = np.ascontiguousarray(inputs["L0_ffn_router_b"].reshape(1, 8))
    g["sh1_4"] = _rhs_ktiles(inputs["L0_ffn_shared_w1"])
    g["sh1b_col"] = _bias_col(inputs["L0_ffn_shared_b1"])
    g["sh2_3"] = _rhs_ktiles(inputs["L0_ffn_shared_w2"])
    g["sh2b_col"] = _bias_col(inputs["L0_ffn_shared_b2"])
    g["up3"] = _rhs_ktiles(inputs["L0_ffn_up_w"])
    g["upb_row"] = np.ascontiguousarray(inputs["L0_ffn_up_b"].reshape(1, D))
    g["core3"] = _rhs_ktiles(inputs["L0_ffn_core_w"])
    g["coreb_row"] = np.ascontiguousarray(inputs["L0_ffn_core_b"].reshape(1, D))
    g["b2cols"] = np.concatenate(
        [_bias_col(inputs[f"L0_ffn_e{e}_b2"]) for e in range(E)], axis=1)
    g["b2stack"] = np.ascontiguousarray(
        np.stack([inputs[f"L0_ffn_e{e}_b2"] for e in range(E)]))
    for e in range(E):
        g[f"e{e}w1_4"] = _rhs_ktiles(inputs[f"L0_ffn_e{e}_w1"])
        g[f"e{e}b1_col"] = _bias_col(inputs[f"L0_ffn_e{e}_b1"])
        g[f"e{e}w2_3"] = _rhs_ktiles(inputs[f"L0_ffn_e{e}_w2"])
    for k in ("dmaskT", "dqb", "dkcol2", "ident_bf", "ident_f32", "ones_row_bf",
              "iota8", "negcost", "epscol", "sel8"):
        g[k] = c[k]

    x = inputs["L0_attn_x"]
    in_maps = []
    for ci in range(NCORES):
        m = dict(g)
        m["x"] = np.ascontiguousarray(
            x[ci * B_PER_CORE:(ci + 1) * B_PER_CORE].reshape(NTOK, D))
        in_maps.append(m)
    return in_maps


def kernel(**inputs):
    in_maps = _prep_inputs(inputs)
    nc = build_nc(debug=DEBUG)
    res = run_bass_kernel_spmd(nc, in_maps, core_ids=list(range(NCORES)))
    out = np.concatenate([r["out"] for r in res.results], axis=0)
    return out.reshape(16, SEQ, D)
